# version 06
# baseline (speedup 1.0000x reference)
import numpy as np
from contextlib import ExitStack

import concourse.bass as bass
import concourse.mybir as mybir
from concourse.bass_utils import run_bass_kernel_spmd

F32 = mybir.dt.float32
BF16 = mybir.dt.bfloat16
I16 = mybir.dt.int16
AF = mybir.ActivationFunctionType
ALU = mybir.AluOpType
ET = mybir.EngineType

B, T, D = 4, 4096, 2048
E, EH, S, R, DC = 4096, 2048, 16, 128, 4
NM = 16384
SEG = 512
NSEG = T // SEG
NT = 16
NPASS = 4
NSB = 8
EPS = 1e-5
GA, GB = 0.7978845608028654, 0.044715
RG8 = [[0, 1, 2, 3, 4, 5, 6, 7]]


def _rho_order():
    q = np.arange(D)
    return (2 * (128 * (q >> 8) + (q & 127)) + ((q >> 7) & 1)).astype(np.int64)


class Sem:
    def __init__(self, nc, es, name):
        self.h = es.enter_context(nc.semaphore(name))
        self.val = 0

    def inc(self, instr, n=1):
        instr.then_inc(self.h, n)
        self.val += n
        return self.val

    def incd(self, eng, n=1):
        eng.drain().then_inc(self.h, n)
        self.val += n
        return self.val


def build_nc():
    nc = bass.Bass("TRN2", target_bir_lowering=False)

    xT = nc.dram_tensor("xT", [D, T], F32, kind="ExternalInput").ap()
    winT = nc.dram_tensor("winT", [D, 2 * EH], F32, kind="ExternalInput").ap()
    wxT = nc.dram_tensor("wxT", [EH, 176], F32, kind="ExternalInput").ap()
    wdtT = nc.dram_tensor("wdtT", [R, EH], F32, kind="ExternalInput").ap()
    woutT = nc.dram_tensor("woutT", [EH, D], F32, kind="ExternalInput").ap()
    convw = nc.dram_tensor("convw", [128, NT * DC], F32, kind="ExternalInput").ap()
    convb = nc.dram_tensor("convb", [128, NT], F32, kind="ExternalInput").ap()
    bdt = nc.dram_tensor("bdt", [128, NT], F32, kind="ExternalInput").ap()
    alog = nc.dram_tensor("alog", [128, NT * S], F32, kind="ExternalInput").ap()
    dvec = nc.dram_tensor("dvec", [128, NT], F32, kind="ExternalInput").ap()
    gpk = nc.dram_tensor("gpk", [1024, 2048], I16, kind="ExternalInput").ap()
    upk = nc.dram_tensor("upk", [1024, 2048], I16, kind="ExternalInput").ap()
    dpk = nc.dram_tensor("dpk", [1024, 2048], I16, kind="ExternalInput").ap()
    gsc = nc.dram_tensor("gsc", [1024, 2048], F32, kind="ExternalInput").ap()
    usc = nc.dram_tensor("usc", [1024, 2048], F32, kind="ExternalInput").ap()
    dsc = nc.dram_tensor("dsc", [1024, 2048], F32, kind="ExternalInput").ap()
    ident = nc.dram_tensor("ident", [128, 128], F32, kind="ExternalInput").ap()
    out_h = nc.dram_tensor("out", [2048, D], F32, kind="ExternalOutput").ap()

    RGP = [[0, 1], [2, 3], [4, 5], [6, 7]]
    ag_xd_in = [nc.dram_tensor(f"ag_xd_in{s}", [176, SEG], F32, kind="Internal").ap()
                for s in range(NSEG)]
    ag_xd_out = [nc.dram_tensor(f"ag_xd_out{s}", [176, SEG], F32,
                                kind="Internal").ap()
                 for s in range(NSEG)]
    ag_wo_in = [nc.dram_tensor(f"ag_wo_in{s}", [D, SEG], BF16, kind="Internal").ap()
                for s in range(NSEG)]
    ag_wo_out = [nc.dram_tensor(f"ag_wo_out{s}", [D, SEG], BF16,
                                kind="Internal").ap()
                 for s in range(NSEG)]
    ag_g_in = nc.dram_tensor("ag_g_in", [2048, 2048], BF16, kind="Internal").ap()
    ag_u_in = nc.dram_tensor("ag_u_in", [2048, 2048], BF16, kind="Internal").ap()
    ag_d_in = nc.dram_tensor("ag_d_in", [2048, 2048], BF16, kind="Internal").ap()
    WgD = nc.dram_tensor("WgD", [NM, 2048], BF16, kind="Internal").ap()
    WuD = nc.dram_tensor("WuD", [NM, 2048], BF16, kind="Internal").ap()
    WdD = nc.dram_tensor("WdD", [NM, 2048], BF16, kind="Internal").ap()
    resid_d = nc.dram_tensor("resid_d", [D, T], F32, kind="Internal").ap()
    h2_d = nc.dram_tensor("h2_d", [D, T], BF16, kind="Internal").ap()

    es = ExitStack()
    sb = lambda name, shape, dt: es.enter_context(nc.sbuf_tensor(name, shape, dt))
    with es:
        sxn = sb("sxn", [128, NT * SEG], BF16)
        su = sb("su", [128, NT * SEG], BF16)
        sdt = sb("sdt", [128, NT * SEG], BF16)
        sxs = sb("sxs", [128, NT * (SEG + 3)], BF16)
        arA = sb("arA", [128, NT * 2048], BF16)
        arB = sb("arB", [128, NT * SEG], F32)
        sconvw = sb("sconvw", [128, NT * DC], F32)
        sconvb = sb("sconvb", [128, NT], F32)
        sbdt = sb("sbdt", [128, NT], F32)
        sdvec = sb("sdvec", [128, NT], F32)
        sA = sb("sA", [128, NT * S], F32)
        swx = sb("swx", [128, NT * 176], BF16)
        swdt = sb("swdt", [128, EH], BF16)
        sident = sb("sident", [128, 128], F32)
        sones = sb("sones", [128, 1], BF16)
        sonef = sb("sonef", [1, 128], F32)
        scarry = sb("scarry", [128, NT * S], F32)
        sxpa = sb("sxpa", [128, SEG], F32)
        sxpb = sb("sxpb", [48, SEG], F32)
        sxda = sxpa
        sxdb = sxpb
        sdtr = sb("sdtr", [128, SEG], BF16)
        st_mu = sb("st_mu", [1, SEG], F32)
        st_e2 = sb("st_e2", [1, SEG], F32)
        st_t1 = sb("st_t1", [1, SEG], F32)
        st_a = sb("st_a", [1, SEG], F32)
        st_b = sb("st_b", [1, SEG], F32)
        sab_a = sb("sab_a", [128, SEG], F32)
        sab_b = sb("sab_b", [128, SEG], F32)
        sct = sb("sct", [128, 2 * SEG], F32)
        stmp = sb("stmp", [128, 2 * SEG], F32)
        szs = sb("szs", [128, 2 * SEG], BF16)
        stmp2 = sct
        sevac = szs


        swin = arA[:]
        sh = arA[:, 0:S * SEG]
        sdA = [arA[:, S * SEG:2 * S * SEG],
               arA[:, 2 * S * SEG:3 * S * SEG]]
        sgx = [arA[:, 0:SEG], arA[:, SEG:2 * SEG]]
        sux = [arA[:, 2 * SEG:3 * SEG], arA[:, 3 * SEG:4 * SEG]]
        sqx = [arA[:, 4 * SEG:5 * SEG], arA[:, 5 * SEG:6 * SEG]]
        srx = [arA[:, 6 * SEG:7 * SEG], arA[:, 7 * SEG:8 * SEG]]
        sinter = arA[:, 8 * SEG:(8 + NT) * SEG]
        sw_t = arA[:, 3 * S * SEG:3 * S * SEG + SEG]
        sdbu = arA[:, 3 * S * SEG + SEG:3 * S * SEG + 2 * SEG]

        arBb = arB[:].bitcast(BF16)
        sBbc = arBb[:, 0:S * SEG]
        sCbc = arBb[:, S * SEG:2 * S * SEG]
        sxf = arB[:]
        sacc = arB[:]
        arBi = arB[:].bitcast(I16)
        sdk = arBi[:, 0:1024]
        sdn = [arBi[:, 1024:2048], arBi[:, 2048:3072]]
        sdm = arBi[:, 3072:4096]
        sdq = arBi[:, 4096:5120]
        sdz = arBi[:, 5120:6144]
        sdsc = arB[:, 3072:4096]
        sdv = [arBb[:, 8192:9216], arBb[:, 9216:10240]]
        smine = sdt[:]
        sresb = sdt[:]
        swd = sdt[:]
        swg = [sxs[:, 0:NT * 128], sxs[:, NT * 128:2 * NT * 128]]
        swu = [sxs[:, 2 * NT * 128:3 * NT * 128], sxs[:, 3 * NT * 128:4 * NT * 128]]
        sh2t = sxn[:]
        sout = su[:].bitcast(F32)

        pb = [es.enter_context(nc.psum_tensor(f"pb{i}", [128, 512], F32))
              for i in range(8)]

        qS = Sem(nc, es, "qS")
        qG = Sem(nc, es, "qG")
        sPE = Sem(nc, es, "sPE")
        sACT = Sem(nc, es, "sACT")
        sDVE = Sem(nc, es, "sDVE")
        sST = Sem(nc, es, "sST")
        sCCg = Sem(nc, es, "sCCg")
        sCCu = Sem(nc, es, "sCCu")
        sCCd = Sem(nc, es, "sCCd")
        sCCx = [Sem(nc, es, f"sCCx{i}") for i in range(NSEG)]
        sCCw = [Sem(nc, es, f"sCCw{i}") for i in range(NSEG)]

        seps = sb("seps", [128, 1], F32)
        sone_b = sb("sone_b", [128, 1], F32)

        blk = es.enter_context(nc.Block())

        bank_ready = [(sACT, 0) for _ in range(8)]

        def wait_bank(t, b):
            sem, val = bank_ready[b]
            if val > 0:
                t.wait_ge(sem.h, val)

        def seg_sl(s):
            return slice(SEG * s, SEG * (s + 1))

        def dm3(dram_ap, w):
            return dram_ap.rearrange("(i p) t -> p i t", p=128)

        def sb3(sbuf_ap, w):
            return sbuf_ap.rearrange("p (i t) -> p i t", t=w)

        @blk.sync
        def _(sync):
            for src, dst in [(convw, sconvw), (convb, sconvb), (bdt, sbdt),
                             (dvec, sdvec), (alog, sA), (ident, sident)]:
                sync.dma_start(out=dst[:], in_=src[:]).then_inc(qS.h, 16)
                qS.val += 16

        @blk.gpsimd
        def _(g):
            g.dma_start(out=sb3(swx[:], 176), in_=dm3(wxT[:], 176)
                        ).then_inc(qG.h, 16)
            g.dma_start(out=swdt[:], in_=wdtT[:]).then_inc(qG.h, 16)
            qG.val += 32

        @blk.vector
        def _(v):
            v.memset(sones[:], 1.0)
            v.memset(sonef[:], 1.0)
            v.memset(seps[:], EPS)
            v.memset(sone_b[:], 1.0)
            v.memset(scarry[:], 0.0)
            hdr = sxs[:].rearrange("p (i w) -> p i w", w=SEG + 3)
            v.memset(hdr[:, :, 0:3], 0.0)
            sDVE.incd(v)
        init_dve = sDVE.val

        @blk.scalar
        def _(sc):
            sc.wait_ge(qS.h, qS.val)
            sc.wait_ge(sDVE.h, init_dve)
            sc.activation(sA[:], sA[:], AF.Exp)
            sc.activation(sA[:], sA[:], AF.Copy, scale=-1.0)
            sACT.incd(sc)

        dec_dve = [0]
        for mat, (pk_h, sc_h, agin) in (("g", (gpk, gsc, ag_g_in)),
                                        ("u", (upk, usc, ag_u_in)),
                                        ("d", (dpk, dsc, ag_d_in))):
            for t8 in range(8):
                for ch in range(2):
                    csl = slice(1024 * ch, 1024 * (ch + 1))

                    @blk.gpsimd
                    def _(g, t8=t8, csl=csl):
                        if dec_dve[0] > 0:
                            g.wait_ge(sDVE.h, dec_dve[0])
                        g.dma_start(out=sdk[:], in_=pk_h[128 * t8:128 * (t8 + 1), csl]
                                    ).then_inc(qG.h, 16)
                        g.dma_start(out=sdsc[:],
                                    in_=sc_h[128 * t8:128 * (t8 + 1), csl]
                                    ).then_inc(qG.h, 16)
                        qG.val += 32

                    @blk.vector
                    def _(v, t8=t8):
                        v.wait_ge(qG.h, qG.val)
                        v.tensor_scalar(sdn[0], sdk[:], 15, None, ALU.bitwise_and)
                        v.tensor_scalar(sdn[1], sdk[:], 4, None,
                                        ALU.logical_shift_right)
                        for nb in range(2):
                            c = sdn[nb]
                            v.tensor_scalar(sdm[:], c, 7, None, ALU.bitwise_and)
                            v.tensor_scalar(sdz[:], sdm[:], 2, None, ALU.is_ge)
                            v.tensor_tensor(sdq[:], sdm[:], sdz[:], ALU.add)
                            v.tensor_scalar(sdq[:], sdq[:], 0x40, 0x3EC0,
                                            ALU.mult, ALU.add)
                            v.tensor_scalar(sdz[:], sdm[:], 1, None, ALU.is_ge)
                            v.tensor_tensor(sdq[:], sdq[:], sdz[:], ALU.mult)
                            v.tensor_scalar(sdz[:], c, 8, 12, ALU.bitwise_and,
                                            ALU.logical_shift_left)
                            v.tensor_tensor(sdq[:], sdq[:], sdz[:], ALU.add)
                            last = v.tensor_tensor(sdv[nb], sdq[:].bitcast(BF16),
                                                   sdsc[:], ALU.mult)
                            if nb == 1:
                                last
                                sDVE.incd(v)
                        dec_dve[0] = sDVE.val

                    @blk.gpsimd
                    def _(g, t8=t8, ch=ch, mat=mat):
                        g.wait_ge(sDVE.h, dec_dve[0])
                        for nb in range(2):
                            if mat != "d":
                                kt = 2 * t8 + nb
                                dst = agin.rearrange("(a p) (i n) -> p a i n",
                                                     p=128, n=128)
                                dst = dst[:, 8 * ch:8 * (ch + 1), kt, :]
                                src = sdv[nb].rearrange("p (a n) -> p a n", n=128)
                            else:
                                r0 = 256 * t8 + 128 * nb
                                dst = agin[r0:r0 + 128, 1024 * ch:1024 * (ch + 1)]
                                src = sdv[nb]
                            g.dma_start(out=dst, in_=src).then_inc(qG.h, 16)
                            qG.val += 16

        @blk.gpsimd
        def _(g):
            g.wait_ge(qG.h, qG.val)
            sCCg.inc(g.collective_compute(
                "AllGather", ALU.bypass, replica_groups=RG8,
                ins=[ag_g_in[:].opt()], outs=[WgD[:].opt()]))
            sCCu.inc(g.collective_compute(
                "AllGather", ALU.bypass, replica_groups=RG8,
                ins=[ag_u_in[:].opt()], outs=[WuD[:].opt()]))
            sCCd.inc(g.collective_compute(
                "AllGather", ALU.bypass, replica_groups=RG8,
                ins=[ag_d_in[:].opt()], outs=[WdD[:].opt()]))

        dec_done_dve = sDVE.val
        dec_done_qg = qG.val

        xn_ready = [0] * NSEG
        u_ready = [0] * NSEG
        hdr_ready = [0] * NSEG
        bc_done = [0] * NSEG
        dt_ready = [0] * NSEG
        scan_done = [0] * NSEG
        p_done = [0] * NSEG
        res_done = [0] * NSEG
        h2_done = [0] * NSEG
        wout_mm_done = [0] * NSEG
        zmm_done = [0] * NSEG
        xdmm_done = [0] * NSEG
        agin_dma = [0] * NSEG
        xdsum_done = [0] * NSEG
        dtmm_done = [0] * NSEG
        dAlast = [0] * NSEG

        def emit_ln(tag, src_bf, sq_buf, dst_bf, s):
            @blk.scalar
            def _(sc):
                sc.activation(sq_buf, src_bf, AF.Square)
                sACT.incd(sc)
            sqv = sACT.val

            @blk.tensor
            def _(t):
                t.wait_ge(sACT.h, sqv)
                wait_bank(t, 6)
                wait_bank(t, 7)
                for i in range(NT):
                    t.matmul(pb[6][0:1, :], sones[:], src_bf[:, 512 * i:512 * (i + 1)],
                             start=(i == 0), stop=(i == NT - 1))
                for i in range(NT):
                    m = t.matmul(pb[7][0:1, :], sones[:],
                                 sq_buf[:, 512 * i:512 * (i + 1)],
                                 start=(i == 0), stop=(i == NT - 1))
                m
                sPE.incd(t)
            stv = sPE.val

            @blk.vector
            def _(v):
                v.wait_ge(sPE.h, stv)
                v.tensor_scalar(st_mu[:], pb[6][0:1, :], 1.0 / D, None, ALU.mult)
                v.tensor_scalar(st_e2[:], pb[7][0:1, :], 1.0 / D, None, ALU.mult)
                v.tensor_tensor(st_t1[:], st_mu[:], st_mu[:], ALU.mult)
                v.tensor_tensor(st_t1[:], st_e2[:], st_t1[:], ALU.subtract)
                sDVE.incd(v)
            varv = sDVE.val

            @blk.scalar
            def _(sc):
                sc.wait_ge(sDVE.h, varv)
                sc.activation(st_t1[:], st_t1[:], AF.Ln, bias=seps[0:1, 0:1])
                sc.activation(st_a[:], st_t1[:], AF.Exp, scale=-0.5)
                sACT.incd(sc)
            rsv = sACT.val

            @blk.vector
            def _(v):
                v.wait_ge(sACT.h, rsv)
                v.tensor_tensor(st_b[:], st_mu[:], st_a[:], ALU.mult)
                v.tensor_scalar(st_b[:], st_b[:], -1.0, None, ALU.mult)
                sDVE.incd(v)
            abv = sDVE.val

            @blk.tensor
            def _(t):
                t.wait_ge(sDVE.h, abv)
                t.matmul(pb[6][:, :], sonef[:], st_a[:], start=True, stop=True)
                m = t.matmul(pb[7][:, :], sonef[:], st_b[:], start=True, stop=True)
                m
                sPE.incd(t)
            bcv = sPE.val

            @blk.scalar
            def _(sc):
                sc.wait_ge(sPE.h, bcv)
                sc.copy(sab_a[:], pb[6][:, :])
                sc.copy(sab_b[:], pb[7][:, :])
                sACT.incd(sc)
            bank_ready[6] = (sACT, sACT.val)
            bank_ready[7] = (sACT, sACT.val)
            abv2 = sACT.val

            @blk.vector
            def _(v):
                v.wait_ge(sACT.h, abv2)
                last = None
                for i in range(NT):
                    dsl = slice(SEG * i, SEG * (i + 1))
                    v.tensor_tensor(dst_bf[:, dsl], src_bf[:, dsl], sab_a[:],
                                    ALU.mult)
                    last = v.tensor_tensor(dst_bf[:, dsl], dst_bf[:, dsl],
                                           sab_b[:], ALU.add)
                last
                sDVE.incd(v)
            return sDVE.val

        for s in range(NSEG):
            @blk.sync
            def _(sync, s=s):
                if s > 0:
                    sync.wait_ge(qG.h, h2_done[s - 1])
                else:
                    sync.wait_ge(sDVE.h, dec_done_dve)
                    sync.wait_ge(qG.h, dec_done_qg)
                src = dm3(xT[:, seg_sl(s)], SEG)
                sync.dma_start(out=sb3(sxf[:, 0:NT * SEG], SEG),
                               in_=src).then_inc(qS.h, 16)
                qS.val += 16
            xload_v = qS.val

            @blk.gpsimd
            def _(g, s=s):
                if s > 0:
                    g.wait_ge(sPE.h, wout_mm_done[s - 1])
                for i in range(NT):
                    g.dma_start(out=swin[:, 2048 * i:2048 * (i + 1)],
                                in_=winT[128 * i:128 * (i + 1), 0:2048]
                                ).then_inc(qG.h, 16)
                    qG.val += 16
            wld_f = qG.val

            @blk.vector
            def _(v, s=s):
                v.wait_ge(qS.h, xload_v)
                if s > 0:
                    v.wait_ge(sPE.h, zmm_done[s - 1])
                    v.wait_ge(qG.h, h2_done[s - 1])
                v.tensor_copy(sxn[:], sxf[:, 0:NT * SEG])
                sDVE.incd(v)
            @blk.scalar
            def _(sc, s=s):
                sc.wait_ge(sDVE.h, sDVE.val)
                if s > 0:
                    sc.wait_ge(sPE.h, wout_mm_done[s - 1])
            xn_ready[s] = emit_ln(f"ln1_{s}", sxn[:], su[:], sxn[:], s)

            for j in range(NT):
                bj = j % 2
                @blk.tensor
                def _(t, s=s, j=j, bj=bj):
                    if j == 0:
                        t.wait_ge(sDVE.h, xn_ready[s])
                        t.wait_ge(qG.h, wld_f)
                        if s > 0:
                            t.wait_ge(sDVE.h, hdr_ready[s - 1])
                    wait_bank(t, bj)
                    for i in range(NT):
                        m = t.matmul(pb[bj][:, :],
                                     swin[:, 2048 * i + 128 * j:2048 * i + 128 * (j + 1)],
                                     sxn[:, 512 * i:512 * (i + 1)],
                                     start=(i == 0), stop=(i == NT - 1))
                    m
                    sPE.incd(t)
                mmv = sPE.val

                @blk.scalar
                def _(sc, s=s, j=j, bj=bj):
                    sc.wait_ge(sPE.h, mmv)
                    sc.copy(sxs[:, (SEG + 3) * j + 3:(SEG + 3) * (j + 1)],
                                     pb[bj][:, :])
                    sACT.incd(sc)
                bank_ready[bj] = (sACT, sACT.val)
                evv = sACT.val

                @blk.vector
                def _(v, s=s, j=j):
                    v.wait_ge(sACT.h, evv)
                    base = (SEG + 3) * j
                    cp = j % 2
                    v.tensor_scalar(sct[:, SEG * cp:SEG * (cp + 1)],
                                    sxs[:, base:base + SEG],
                                    sconvw[:, DC * j:DC * j + 1], None, ALU.mult)
                    for k in range(1, DC):
                        v.scalar_tensor_tensor(
                            sct[:, SEG * cp:SEG * (cp + 1)],
                            sxs[:, base + k:base + k + SEG],
                            sconvw[:, DC * j + k:DC * j + k + 1],
                            sct[:, SEG * cp:SEG * (cp + 1)], ALU.mult, ALU.add)
                    v.tensor_copy(sct[0:1, SEG * cp:SEG * cp + 1],
                                           sct[0:1, SEG * cp:SEG * cp + 1])
                    sDVE.incd(v)
                cvv = sDVE.val

                @blk.scalar
                def _(sc, s=s, j=j):
                    sc.wait_ge(sDVE.h, cvv)
                    cp = j % 2
                    sc.activation(su[:, SEG * j:SEG * (j + 1)],
                                           sct[:, SEG * cp:SEG * (cp + 1)], AF.Silu,
                                           bias=sconvb[:, j:j + 1])
                    sACT.incd(sc)
            u_ready[s] = sACT.val

            @blk.vector
            def _(v, s=s):
                v.wait_ge(sACT.h, u_ready[s])
                x3s = sxs[:].rearrange("p (i w) -> p i w", w=SEG + 3)
                v.tensor_copy(x3s[:, :, 0:3], x3s[:, :, SEG:SEG + 3])
                sDVE.incd(v)
            hdr_ready[s] = sDVE.val

            @blk.tensor
            def _(t, s=s):
                t.wait_ge(sACT.h, u_ready[s])
                wait_bank(t, 2)
                wait_bank(t, 3)
                for i in range(NT):
                    t.matmul(pb[2][:, :], swx[:, 176 * i:176 * i + 128],
                             su[:, 512 * i:512 * (i + 1)],
                             start=(i == 0), stop=(i == NT - 1))
                for i in range(NT):
                    m = t.matmul(pb[3][0:48, :], swx[:, 176 * i + 128:176 * (i + 1)],
                                 su[:, 512 * i:512 * (i + 1)],
                                 start=(i == 0), stop=(i == NT - 1))
                m
                sPE.incd(t)
            xdmm_done[s] = sPE.val

            @blk.scalar
            def _(sc, s=s):
                sc.wait_ge(sPE.h, xdmm_done[s])
                if s > 0:
                    sc.wait_ge(qG.h, bc_done[s - 1])
                sc.copy(sxpa[:], pb[2][:, :])
                sc.copy(sxpb[:], pb[3][0:48, :])
                sACT.incd(sc)
            bank_ready[2] = (sACT, sACT.val)
            bank_ready[3] = (sACT, sACT.val)
            xdev = sACT.val

            @blk.gpsimd
            def _(g, s=s):
                g.wait_ge(sACT.h, xdev)
                g.dma_start(out=ag_xd_in[s][0:128, :], in_=sxpa[:]).then_inc(qG.h, 16)
                g.dma_start(out=ag_xd_in[s][128:176, :], in_=sxpb[:]).then_inc(qG.h, 16)
                qG.val += 32
                agin_dma[s] = qG.val
                g.wait_ge(qG.h, qG.val)
                sCCx[s].inc(g.collective_compute(
                    "AllReduce", ALU.add, replica_groups=RGP,
                    ins=[ag_xd_in[s][:].opt()], outs=[ag_xd_out[s][:].opt()]))

            @blk.gpsimd
            def _(g, s=s):
                g.wait_ge(sCCx[s].h, 1)
                if s > 0:
                    g.wait_ge(sPE.h, dtmm_done[s - 1])
                    g.wait_ge(sDVE.h, res_done[s - 1])
                    g.wait_ge(qG.h, h2_done[s - 1])
                g.dma_start(out=sdtr[:], in_=ag_xd_out[s][0:128, :]
                            ).then_inc(qG.h, 16)
                srcB = ag_xd_out[s][128:144, :].unsqueeze(0).broadcast_to(
                    [128, 16, SEG])
                g.dma_start(out=sBbc.rearrange("p (g t) -> p g t", t=SEG),
                            in_=srcB).then_inc(qG.h, 16)
                srcC = ag_xd_out[s][160:176, :].unsqueeze(0).broadcast_to(
                    [128, 16, SEG])
                g.dma_start(out=sCbc.rearrange("p (g t) -> p g t", t=SEG),
                            in_=srcC).then_inc(qG.h, 16)
                qG.val += 48
            bc_done[s] = qG.val
            xdsum_done[s] = 0

            for j in range(NT):
                bj = 4 + (j % 2)
                jp = j % 2
                @blk.tensor
                def _(t, s=s, j=j, bj=bj):
                    if j == 0:
                        t.wait_ge(qG.h, bc_done[s])
                    wait_bank(t, bj)
                    m = t.matmul(pb[bj][:, :], swdt[:, 128 * j:128 * (j + 1)],
                                 sdtr[:], start=True, stop=True)
                    m
                    sPE.incd(t)
                mmv = sPE.val

                @blk.scalar
                def _(sc, s=s, j=j, bj=bj, jp=jp):
                    sc.wait_ge(sPE.h, mmv)
                    if j >= 2:
                        sc.wait_ge(sDVE.h, dt_tt[jp])
                    tsl = slice(SEG * jp, SEG * (jp + 1))
                    sc.activation(stmp[:, tsl], pb[bj][:, :], AF.Abs,
                                  bias=sbdt[:, j:j + 1])
                    sc.activation(stmp[:, tsl], stmp[:, tsl], AF.Exp, scale=-1.0)
                    sc.activation(stmp[:, tsl], stmp[:, tsl], AF.Ln, bias=sone_b[:, 0:1])
                    sc.activation(stmp2[:, tsl], pb[bj][:, :], AF.Relu,
                                           bias=sbdt[:, j:j + 1])
                    sACT.incd(sc)
                spv = sACT.val
                bank_ready[bj] = (sACT, sACT.val)

                @blk.vector
                def _(v, s=s, j=j, jp=jp):
                    v.wait_ge(sACT.h, spv)
                    tsl = slice(SEG * jp, SEG * (jp + 1))
                    v.tensor_tensor(sdt[:, SEG * j:SEG * (j + 1)],
                                             stmp[:, tsl], stmp2[:, tsl], ALU.add)
                    sDVE.incd(v)
                if j == 0:
                    dt_tt = [0, 0]
                dt_tt[jp] = sDVE.val
            dt_ready[s] = sDVE.val
            dtmm_done[s] = sPE.val

            dA_wr = [0, 0]
            dA_rd = [0, 0]
            for j in range(NT):
                pp = j % 2
                @blk.scalar
                def _(sc, s=s, j=j, pp=pp):
                    if j == 0:
                        sc.wait_ge(sDVE.h, dt_ready[s])
                    if dA_rd[pp] > 0:
                        sc.wait_ge(sDVE.h, dA_rd[pp])
                    last = None
                    for sig in range(S):
                        last = sc.activation(
                            sdA[pp][:, SEG * sig:SEG * (sig + 1)],
                            sdt[:, SEG * j:SEG * (j + 1)], AF.Exp,
                            scale=sA[:, S * j + sig:S * j + sig + 1])
                    last
                    sACT.incd(sc)
                dA_wr[pp] = sACT.val

                @blk.vector
                def _(v, s=s, j=j, pp=pp):
                    if j == 0:
                        v.wait_ge(qG.h, bc_done[s])
                    v.wait_ge(sACT.h, dA_wr[pp])
                    v.tensor_tensor(sw_t[:], sdt[:, SEG * j:SEG * (j + 1)],
                                    su[:, SEG * j:SEG * (j + 1)], ALU.mult)
                    for sig in range(S):
                        ssl = slice(SEG * sig, SEG * (sig + 1))
                        v.tensor_tensor(sdbu[:], sw_t[:], sBbc[:, ssl], ALU.mult)
                        init = 0.0 if s == 0 else \
                            scarry[:, S * j + sig:S * j + sig + 1]
                        v.tensor_tensor_scan(sh[:, ssl], sdA[pp][:, ssl], sdbu[:],
                                             init, ALU.mult, ALU.add)
                    hh = sh.rearrange("p (g t) -> p g t", t=SEG)
                    v.tensor_copy(scarry[:, S * j:S * (j + 1)], hh[:, :, SEG - 1])
                    v.tensor_tensor(sdA[pp], sh, sCbc, ALU.mult)
                    half = S * SEG // 2
                    while half >= SEG:
                        v.tensor_tensor(sdA[pp][:, 0:half], sdA[pp][:, 0:half],
                                        sdA[pp][:, half:2 * half], ALU.add)
                        half //= 2
                    last = v.scalar_tensor_tensor(
                        su[:, SEG * j:SEG * (j + 1)], su[:, SEG * j:SEG * (j + 1)],
                        sdvec[:, j:j + 1], sdA[pp][:, 0:SEG], ALU.mult, ALU.add)
                    last
                    sDVE.incd(v)
                    dA_rd[pp] = sDVE.val
            scan_done[s] = sDVE.val
            dAlast[s] = sACT.val

            @blk.gpsimd
            def _(g, s=s):
                g.wait_ge(sPE.h, xdmm_done[s])
                g.wait_ge(sDVE.h, scan_done[s])
                for i in range(NT):
                    g.dma_start(out=swin[:, 2048 * i:2048 * (i + 1)],
                                in_=winT[128 * i:128 * (i + 1), 2048:4096]
                                ).then_inc(qG.h, 16)
                    qG.val += 16
            zw_v = qG.val

            for j in range(NT):
                bj = j % 2
                jp = j % 2
                @blk.tensor
                def _(t, s=s, j=j, bj=bj):
                    if j == 0:
                        t.wait_ge(qG.h, zw_v)
                    wait_bank(t, bj)
                    for i in range(NT):
                        m = t.matmul(pb[bj][:, :],
                                     swin[:, 2048 * i + 128 * j:2048 * i + 128 * (j + 1)],
                                     sxn[:, 512 * i:512 * (i + 1)],
                                     start=(i == 0), stop=(i == NT - 1))
                    m
                    sPE.incd(t)
                mmv = sPE.val

                @blk.scalar
                def _(sc, s=s, j=j, bj=bj, jp=jp):
                    sc.wait_ge(sPE.h, mmv)
                    if j >= 2:
                        sc.wait_ge(sDVE.h, p_tt[jp])
                    sc.activation(szs[:, SEG * jp:SEG * (jp + 1)],
                                           pb[bj][:, :], AF.Silu)
                    sACT.incd(sc)
                bank_ready[bj] = (sACT, sACT.val)
                zev = sACT.val

                @blk.vector
                def _(v, s=s, j=j, jp=jp):
                    v.wait_ge(sACT.h, zev)
                    if j == 0:
                        v.wait_ge(sDVE.h, scan_done[s])
                    v.tensor_tensor(su[:, SEG * j:SEG * (j + 1)],
                                             su[:, SEG * j:SEG * (j + 1)],
                                             szs[:, SEG * jp:SEG * (jp + 1)],
                                             ALU.mult)
                    sDVE.incd(v)
                if j == 0:
                    p_tt = [0, 0]
                p_tt[jp] = sDVE.val
            p_done[s] = sDVE.val
            zmm_done[s] = sPE.val

            @blk.gpsimd
            def _(g, s=s):
                g.wait_ge(sPE.h, zmm_done[s])
                for i in range(NT):
                    g.dma_start(out=swin[:, 2048 * i:2048 * (i + 1)],
                                in_=woutT[128 * i:128 * (i + 1), :]
                                ).then_inc(qG.h, 16)
                    qG.val += 16
            wow_v = qG.val

            ag_wo_dma = 0
            for mt in range(NT):
                bj = 2 + (mt % 2)
                mp = mt % 2
                @blk.tensor
                def _(t, s=s, mt=mt, bj=bj):
                    if mt == 0:
                        t.wait_ge(qG.h, wow_v)
                        t.wait_ge(sDVE.h, p_done[s])
                    wait_bank(t, bj)
                    for i in range(NT):
                        m = t.matmul(pb[bj][:, :],
                                     swin[:, 2048 * i + 128 * mt:2048 * i + 128 * (mt + 1)],
                                     su[:, 512 * i:512 * (i + 1)],
                                     start=(i == 0), stop=(i == NT - 1))
                    m
                    sPE.incd(t)
                mmv = sPE.val

                @blk.scalar
                def _(sc, s=s, mt=mt, bj=bj, mp=mp):
                    sc.wait_ge(sPE.h, mmv)
                    if mt >= 2:
                        sc.wait_ge(qG.h, wo_dma[mp])
                    sc.copy(sevac[:, SEG * mp:SEG * (mp + 1)],
                                     pb[bj][:, :])
                    sACT.incd(sc)
                bank_ready[bj] = (sACT, sACT.val)
                wev = sACT.val

                @blk.gpsimd
                def _(g, s=s, mt=mt, mp=mp):
                    g.wait_ge(sACT.h, wev)
                    g.dma_start(out=ag_wo_in[s][128 * mt:128 * (mt + 1), :],
                                in_=sevac[:, SEG * mp:SEG * (mp + 1)]
                                ).then_inc(qG.h, 16)
                    qG.val += 16
                if mt == 0:
                    wo_dma = [0, 0]
                wo_dma[mp] = qG.val
            wout_mm_done[s] = sPE.val

            @blk.gpsimd
            def _(g, s=s):
                g.wait_ge(qG.h, qG.val)
                sCCw[s].inc(g.collective_compute(
                    "AllReduce", ALU.add, replica_groups=RGP,
                    ins=[ag_wo_in[s][:].opt()], outs=[ag_wo_out[s][:].opt()]))

            @blk.sync
            def _(sync, s=s):
                sync.wait_ge(sCCw[s].h, 1)
                sync.wait_ge(sACT.h, dAlast[s])
                sync.dma_start(out=sb3(smine[:], SEG),
                               in_=dm3(ag_wo_out[s][:], SEG)).then_inc(qS.h, 16)
                src3 = dm3(xT[:, seg_sl(s)], SEG)
                sync.dma_start(out=sb3(sxf[:, 0:NT * SEG], SEG),
                               in_=src3).then_inc(qS.h, 16)
                qS.val += 32
            agw_read = qS.val

            @blk.vector
            def _(v, s=s):
                v.wait_ge(qS.h, agw_read)
                v.tensor_tensor(sxf[:, 0:NT * SEG], sxf[:, 0:NT * SEG],
                                         smine[:], ALU.add)
                sDVE.incd(v)
            res_done[s] = sDVE.val

            @blk.vector
            def _(v, s=s):
                v.tensor_copy(sresb[:], sxf[:, 0:NT * SEG])
                sDVE.incd(v)
            resb_v = sDVE.val

            @blk.scalar
            def _(sc, s=s):
                sc.wait_ge(sDVE.h, resb_v)
            h2v = emit_ln(f"ln2_{s}", sresb[:], sxn[:], sxn[:], s)

            @blk.gpsimd
            def _(g, s=s):
                g.wait_ge(sDVE.h, h2v)
                dst1 = dm3(resid_d[:, seg_sl(s)], SEG)
                g.dma_start(out=dst1, in_=sb3(sxf[:, 0:NT * SEG], SEG)
                            ).then_inc(qG.h, 16)
                dst2 = dm3(h2_d[:, seg_sl(s)], SEG)
                g.dma_start(out=dst2, in_=sb3(sxn[:], SEG)).then_inc(qG.h, 16)
                qG.val += 32
            h2_done[s] = qG.val

        @blk.sync
        def _(sync):
            sync.wait_ge(sCCg.h, 1)
            sync.wait_ge(sCCu.h, 1)
            sync.wait_ge(sCCd.h, 1)

        gu_mm_prev_pass = 0
        gu_rd = [0, 0]
        swd_use = [0]
        st_prev_pass = 0
        st_prev_dve = 0
        ffn_gate_dve = sDVE.val
        ffn_gate_act = sACT.val
        ffn_gate_pe = sPE.val
        for p in range(NPASS):
            @blk.sync
            def _(sync, p=p):
                sync.wait_ge(qG.h, h2_done[NSEG - 1])
                if p == 0:
                    sync.wait_ge(sDVE.h, ffn_gate_dve)
                    sync.wait_ge(sACT.h, ffn_gate_act)
                    sync.wait_ge(sPE.h, ffn_gate_pe)
                else:
                    sync.wait_ge(sPE.h, gu_mm_prev_pass)
                    sync.wait_ge(sDVE.h, st_prev_dve)
                pid = nc.partition_id([ET.SP])
                for r in sync.Switch(pid, 8):
                    sseg = 2 * p + (int(r) % 2)
                    src = dm3(h2_d[:, SEG * sseg:SEG * (sseg + 1)], SEG)
                    sync.dma_start(out=sb3(sh2t[:], SEG), in_=src
                                   ).then_inc(qS.h, 16)
                    src2 = dm3(resid_d[:, SEG * sseg:SEG * (sseg + 1)], SEG)
                    sync.dma_start(out=sb3(sacc[:, 0:NT * SEG], SEG), in_=src2
                                   ).then_inc(qS.h, 16)
                qS.val += 32
            h2t_v = qS.val

            dmm_last_sb = 0
            wg_use = [0, 0]
            for sbi in range(NSB):
                inter_last = 0
                for j in range(NT):
                    ntile = NT * sbi + j
                    pp2 = ntile % 2

                    @blk.gpsimd
                    def _(g, p=p, ntile=ntile, pp2=pp2):
                        if wg_use[pp2] > 0:
                            g.wait_ge(sPE.h, wg_use[pp2])
                        g.dma_start(out=swg[pp2],
                                    in_=WgD[128 * ntile:128 * (ntile + 1), :]
                                    ).then_inc(qG.h, 16)
                        g.dma_start(out=swu[pp2],
                                    in_=WuD[128 * ntile:128 * (ntile + 1), :]
                                    ).then_inc(qG.h, 16)
                        qG.val += 32
                    wld_v = qG.val

                    bg, bu = 0 + pp2, 2 + pp2
                    @blk.tensor
                    def _(t, p=p, pp2=pp2, bg=bg, bu=bu, wld_v=wld_v):
                        t.wait_ge(qG.h, wld_v)
                        t.wait_ge(qS.h, h2t_v)
                        wait_bank(t, bg)
                        wait_bank(t, bu)
                        for i in range(NT):
                            t.matmul(pb[bg][:, :], swg[pp2][:, 128 * i:128 * (i + 1)],
                                     sh2t[:, 512 * i:512 * (i + 1)],
                                     start=(i == 0), stop=(i == NT - 1))
                        for i in range(NT):
                            m = t.matmul(pb[bu][:, :],
                                         swu[pp2][:, 128 * i:128 * (i + 1)],
                                         sh2t[:, 512 * i:512 * (i + 1)],
                                         start=(i == 0), stop=(i == NT - 1))
                        m
                        sPE.incd(t)
                    wg_use[pp2] = sPE.val
                    gumm_v = sPE.val

                    @blk.scalar
                    def _(sc, j=j, bg=bg, bu=bu, pp2=pp2):
                        sc.wait_ge(sPE.h, gumm_v)
                        if gu_rd[pp2] > 0:
                            sc.wait_ge(sDVE.h, gu_rd[pp2])
                        sc.copy(sgx[pp2], pb[bg][:, :])
                        sc.activation(sqx[pp2], pb[bg][:, :], AF.Square)
                        sc.copy(sux[pp2], pb[bu][:, :])
                        sACT.incd(sc)
                    guev_v = sACT.val
                    bank_ready[bg] = (sACT, sACT.val)
                    bank_ready[bu] = (sACT, sACT.val)

                    @blk.vector
                    def _(v, j=j, pp2=pp2, sbi=sbi):
                        v.wait_ge(sACT.h, guev_v)
                        if sbi > 0 and j == 0:
                            v.wait_ge(sPE.h, dmm_last_sb)
                        v.tensor_scalar(srx[pp2], sqx[pp2], GB, 1.0,
                                        ALU.mult, ALU.add)
                        v.tensor_tensor(srx[pp2], sgx[pp2], srx[pp2],
                                                 ALU.mult)
                        sDVE.incd(v)
                    p3_v = sDVE.val

                    @blk.scalar
                    def _(sc, pp2=pp2):
                        sc.wait_ge(sDVE.h, p3_v)
                        sc.activation(srx[pp2], srx[pp2], AF.Tanh,
                                               scale=GA)
                        sACT.incd(sc)
                    th_v = sACT.val

                    @blk.vector
                    def _(v, j=j, pp2=pp2):
                        v.wait_ge(sACT.h, th_v)
                        v.tensor_scalar(srx[pp2], srx[pp2], 0.5, 0.5,
                                        ALU.mult, ALU.add)
                        v.tensor_tensor(sgx[pp2], sgx[pp2], sux[pp2], ALU.mult)
                        last = v.tensor_tensor(sinter[:, SEG * j:SEG * (j + 1)],
                                               sgx[pp2], srx[pp2], ALU.mult)
                        last
                        sDVE.incd(v)
                    if sbi == 0 and j == 0 and p == 0:
                        pass
                    gu_rd[pp2] = sDVE.val
                    inter_last = sDVE.val

                for mg in range(4):
                    @blk.gpsimd
                    def _(g, p=p, sbi=sbi, mg=mg):
                        if swd_use[0] > 0:
                            g.wait_ge(sPE.h, swd_use[0])
                        for j in range(NT):
                            nt2 = NT * sbi + j
                            g.dma_start(
                                out=swd[:, SEG * j:SEG * (j + 1)],
                                in_=WdD[128 * nt2:128 * (nt2 + 1),
                                        512 * mg:512 * (mg + 1)]
                                ).then_inc(qG.h, 16)
                        qG.val += 16 * NT
                    dwl_v = qG.val

                    for mt in range(4):
                        bd = 4 + (mt % 2)
                        mtg = 4 * mg + mt
                        @blk.tensor
                        def _(t, mg=mg, mt=mt, bd=bd, dwl_v=dwl_v,
                              inter_last=inter_last):
                            if mt == 0:
                                t.wait_ge(qG.h, dwl_v)
                                t.wait_ge(sDVE.h, inter_last)
                            wait_bank(t, bd)
                            for j in range(NT):
                                m = t.matmul(
                                    pb[bd][:, :],
                                    swd[:, SEG * j + 128 * mt:SEG * j + 128 * (mt + 1)],
                                    sinter[:, 512 * j:512 * (j + 1)],
                                    start=(j == 0), stop=(j == NT - 1))
                            m
                            sPE.incd(t)
                        dmm_v = sPE.val
                        swd_use[0] = sPE.val

                        @blk.vector
                        def _(v, mtg=mtg, bd=bd, dmm_v=dmm_v, sbi=sbi,
                              h2t_v=h2t_v):
                            v.wait_ge(sPE.h, dmm_v)
                            if sbi == 0:
                                v.wait_ge(qS.h, h2t_v)
                            v.tensor_tensor(
                                sacc[:, SEG * mtg:SEG * (mtg + 1)],
                                pb[bd][:, :],
                                sacc[:, SEG * mtg:SEG * (mtg + 1)], ALU.add)
                            sDVE.incd(v)
                        bank_ready[bd] = (sDVE, sDVE.val)
                dmm_last_sb = sPE.val
            fin_v = sDVE.val

            for th in range(2):
                for tt in range(2):
                    ttg = 2 * th + tt
                    for mt in range(NT):
                        @blk.tensor
                        def _(t, p=p, ttg=ttg, mt=mt, fin_v=fin_v):
                            if ttg == 0 and mt == 0:
                                t.wait_ge(sDVE.h, fin_v)
                            wait_bank(t, 6)
                            m = t.transpose(
                                pb[6][:, 0:128],
                                sacc[:, SEG * mt + 128 * ttg:SEG * mt + 128 * (ttg + 1)],
                                sident[:])
                            m
                            sPE.incd(t)
                        tp_v = sPE.val

                        @blk.vector
                        def _(v, p=p, th=th, tt=tt, mt=mt, tp_v=tp_v):
                            v.wait_ge(sPE.h, tp_v)
                            if th > 0 and tt == 0 and mt == 0:
                                v.wait_ge(sST.h, sST.val)
                            colbase = D * tt + 256 * (mt // 2) + (mt % 2)
                            v.tensor_copy(
                                sout[:, colbase:colbase + 255:2], pb[6][:, 0:128])
                            sDVE.incd(v)
                        bank_ready[6] = (sDVE, sDVE.val)
                st_v = sDVE.val

                @blk.gpsimd
                def _(g, p=p, th=th, st_v=st_v):
                    g.wait_ge(sDVE.h, st_v)
                    r0 = 512 * p + 256 * th
                    g.dma_start(out=out_h[r0:r0 + 128, :], in_=sout[:, 0:D]
                                ).then_inc(sST.h, 16)
                    g.dma_start(out=out_h[r0 + 128:r0 + 256, :], in_=sout[:, D:2 * D]
                                ).then_inc(sST.h, 16)
                    sST.val += 32
            st_prev_pass = sST.val
            st_prev_dve = sDVE.val
            gu_mm_prev_pass = sPE.val

        @blk.gpsimd
        def _(g):
            g.wait_ge(sST.h, sST.val)

    return nc


_NC_CACHE = None


def kernel(L0_attn_x, L0_attn_W_in, L0_attn_conv_w, L0_attn_conv_b, L0_attn_W_x,
           L0_attn_W_dt, L0_attn_b_dt, L0_attn_A_log, L0_attn_D, L0_attn_W_out,
           L0_ffn_gate_W_packed, L0_ffn_gate_scales, L0_ffn_up_W_packed,
           L0_ffn_up_scales, L0_ffn_down_W_packed, L0_ffn_down_scales):
    global _NC_CACHE
    x = np.asarray(L0_attn_x)
    d_order = _rho_order()

    WinT = np.ascontiguousarray(np.asarray(L0_attn_W_in).T)[d_order]
    WxT = np.ascontiguousarray(np.asarray(L0_attn_W_x).T)
    WdtT = np.ascontiguousarray(np.asarray(L0_attn_W_dt).T)
    WoutT = np.ascontiguousarray(np.asarray(L0_attn_W_out).T)[:, d_order]
    convw = np.asarray(L0_attn_conv_w)
    convb = np.asarray(L0_attn_conv_b)
    b_dt = np.asarray(L0_attn_b_dt)
    A_log = np.asarray(L0_attn_A_log)
    Dv = np.asarray(L0_attn_D)

    def t16(a_i32):
        v = np.ascontiguousarray(a_i32).view(np.int16)[:, 0::2]
        return np.ascontiguousarray(v.T)

    gpkT = t16(np.asarray(L0_ffn_gate_W_packed))
    upkT = t16(np.asarray(L0_ffn_up_W_packed))
    dpkT = t16(np.asarray(L0_ffn_down_W_packed))
    gscT = np.ascontiguousarray(np.asarray(L0_ffn_gate_scales).T)
    uscT = np.ascontiguousarray(np.asarray(L0_ffn_up_scales).T)
    dscT = np.ascontiguousarray(np.asarray(L0_ffn_down_scales).T)
    dpkT = dpkT[:, d_order]
    dscT = dscT[:, d_order]

    ident = np.eye(128, dtype=np.float32)

    def dmaj(vec_half):
        return np.ascontiguousarray(vec_half.reshape(NT, 128).T)

    in_maps = []
    for c in range(8):
        beta, rho = c // 2, c % 2
        dsl = slice(rho * EH, (rho + 1) * EH)
        xTc = np.ascontiguousarray(x[beta].T[d_order])
        cols = np.r_[rho * EH:(rho + 1) * EH, E + rho * EH:E + (rho + 1) * EH]
        winT_c = np.ascontiguousarray(WinT[:, cols])
        wx = WxT[dsl]
        wxT_c = np.ascontiguousarray(np.concatenate(
            [wx[:, :144], np.zeros((EH, 16), np.float32), wx[:, 144:]], axis=1))
        nsl = slice(2048 * c, 2048 * (c + 1))
        in_maps.append(dict(
            xT=xTc, winT=winT_c, wxT=wxT_c,
            wdtT=np.ascontiguousarray(WdtT[:, dsl]),
            woutT=np.ascontiguousarray(WoutT[dsl]),
            convw=np.ascontiguousarray(
                convw[dsl].reshape(NT, 128, DC).transpose(1, 0, 2).reshape(
                    128, NT * DC)),
            convb=dmaj(convb[dsl]), bdt=dmaj(b_dt[dsl]),
            alog=np.ascontiguousarray(
                A_log[dsl].reshape(NT, 128, S).transpose(1, 0, 2).reshape(
                    128, NT * S)),
            dvec=dmaj(Dv[dsl]),
            gpk=np.ascontiguousarray(gpkT[:, nsl][:, d_order]),
            upk=np.ascontiguousarray(upkT[:, nsl][:, d_order]),
            dpk=np.ascontiguousarray(dpkT[1024 * c:1024 * (c + 1), :]),
            gsc=np.ascontiguousarray(
                np.repeat(gscT[:, nsl][:, d_order], 16, axis=0)),
            usc=np.ascontiguousarray(
                np.repeat(uscT[:, nsl][:, d_order], 16, axis=0)),
            dsc=np.ascontiguousarray(
                np.repeat(dscT[64 * c:64 * (c + 1), :], 16, axis=0)),
            ident=ident,
        ))

    if _NC_CACHE is None:
        _NC_CACHE = build_nc()
    res = run_bass_kernel_spmd(_NC_CACHE, in_maps, core_ids=list(range(8)))

    out = np.empty((B, T, D), dtype=np.float32)
    for c in range(8):
        beta, rho = c // 2, c % 2
        oc = res.results[c]["out"]
        for p in range(NPASS):
            s = 2 * p + rho
            out[beta, SEG * s:SEG * (s + 1), :] = oc[SEG * p:SEG * (p + 1), :]
    return out


# version 07
# speedup vs baseline: 1.0402x; 1.0402x over previous
import numpy as np
from contextlib import ExitStack

import concourse.bass as bass
import concourse.mybir as mybir
from concourse.bass_utils import run_bass_kernel_spmd

F32 = mybir.dt.float32
BF16 = mybir.dt.bfloat16
I16 = mybir.dt.int16
AF = mybir.ActivationFunctionType
ALU = mybir.AluOpType
ET = mybir.EngineType

B, T, D = 4, 4096, 2048
E, EH, S, R, DC = 4096, 2048, 16, 128, 4
NM = 16384
SEG = 512
NSEG = T // SEG
NT = 16
NPASS = 4
NSB = 8
EPS = 1e-5
GA, GB = 0.7978845608028654, 0.044715
RG8 = [[0, 1, 2, 3, 4, 5, 6, 7]]


def _rho_order():
    q = np.arange(D)
    return (2 * (128 * (q >> 8) + (q & 127)) + ((q >> 7) & 1)).astype(np.int64)


class Sem:
    def __init__(self, nc, es, name):
        self.h = es.enter_context(nc.semaphore(name))
        self.val = 0

    def inc(self, instr, n=1):
        instr.then_inc(self.h, n)
        self.val += n
        return self.val

    def incd(self, eng, n=1):
        eng.drain().then_inc(self.h, n)
        self.val += n
        return self.val


def build_nc():
    nc = bass.Bass("TRN2", target_bir_lowering=False)

    xT = nc.dram_tensor("xT", [D, T], F32, kind="ExternalInput").ap()
    winT = nc.dram_tensor("winT", [D, 2 * EH], F32, kind="ExternalInput").ap()
    wxT = nc.dram_tensor("wxT", [EH, 176], F32, kind="ExternalInput").ap()
    wdtT = nc.dram_tensor("wdtT", [R, EH], F32, kind="ExternalInput").ap()
    woutT = nc.dram_tensor("woutT", [EH, D], F32, kind="ExternalInput").ap()
    convw = nc.dram_tensor("convw", [128, NT * DC], F32, kind="ExternalInput").ap()
    convb = nc.dram_tensor("convb", [128, NT], F32, kind="ExternalInput").ap()
    bdt = nc.dram_tensor("bdt", [128, NT], F32, kind="ExternalInput").ap()
    alog = nc.dram_tensor("alog", [128, NT * S], F32, kind="ExternalInput").ap()
    dvec = nc.dram_tensor("dvec", [128, NT], F32, kind="ExternalInput").ap()
    gpk = nc.dram_tensor("gpk", [1024, 2048], I16, kind="ExternalInput").ap()
    upk = nc.dram_tensor("upk", [1024, 2048], I16, kind="ExternalInput").ap()
    dpk = nc.dram_tensor("dpk", [1024, 2048], I16, kind="ExternalInput").ap()
    gsc = nc.dram_tensor("gsc", [1024, 2048], F32, kind="ExternalInput").ap()
    usc = nc.dram_tensor("usc", [1024, 2048], F32, kind="ExternalInput").ap()
    dsc = nc.dram_tensor("dsc", [1024, 2048], F32, kind="ExternalInput").ap()
    ident = nc.dram_tensor("ident", [128, 128], F32, kind="ExternalInput").ap()
    out_h = nc.dram_tensor("out", [2048, D], F32, kind="ExternalOutput").ap()

    RGP = [[0, 1], [2, 3], [4, 5], [6, 7]]
    ag_xd_in = [nc.dram_tensor(f"ag_xd_in{s}", [176, SEG], F32, kind="Internal").ap()
                for s in range(NSEG)]
    ag_xd_out = [nc.dram_tensor(f"ag_xd_out{s}", [176, SEG], F32,
                                kind="Internal").ap()
                 for s in range(NSEG)]
    ag_wo_in = [nc.dram_tensor(f"ag_wo_in{s}", [D, SEG], BF16, kind="Internal").ap()
                for s in range(NSEG)]
    ag_wo_out = [nc.dram_tensor(f"ag_wo_out{s}", [D, SEG], BF16,
                                kind="Internal").ap()
                 for s in range(NSEG)]
    ag_g_in = nc.dram_tensor("ag_g_in", [2048, 2048], BF16, kind="Internal").ap()
    ag_u_in = nc.dram_tensor("ag_u_in", [2048, 2048], BF16, kind="Internal").ap()
    ag_d_in = nc.dram_tensor("ag_d_in", [2048, 2048], BF16, kind="Internal").ap()
    WgD = nc.dram_tensor("WgD", [NM, 2048], BF16, kind="Internal").ap()
    WuD = nc.dram_tensor("WuD", [NM, 2048], BF16, kind="Internal").ap()
    WdD = nc.dram_tensor("WdD", [NM, 2048], BF16, kind="Internal").ap()
    resid_d = nc.dram_tensor("resid_d", [D, T], F32, kind="Internal").ap()
    h2_d = nc.dram_tensor("h2_d", [D, T], BF16, kind="Internal").ap()

    es = ExitStack()
    sb = lambda name, shape, dt: es.enter_context(nc.sbuf_tensor(name, shape, dt))
    with es:
        sxn = sb("sxn", [128, NT * SEG], BF16)
        su = sb("su", [128, NT * SEG], BF16)
        sdt = sb("sdt", [128, NT * SEG], BF16)
        sxs = sb("sxs", [128, NT * (SEG + 3)], BF16)
        arA = sb("arA", [128, NT * 2048], BF16)
        arB = sb("arB", [128, NT * SEG], F32)
        sconvw = sb("sconvw", [128, NT * DC], F32)
        sconvb = sb("sconvb", [128, NT], F32)
        sconvbn = sb("sconvbn", [128, NT], F32)
        sbdt = sb("sbdt", [128, NT], F32)
        sdvec = sb("sdvec", [128, NT], F32)
        sA = sb("sA", [128, NT * S], F32)
        swx = sb("swx", [128, NT * 176], BF16)
        swdt = sb("swdt", [128, EH], BF16)
        sident = sb("sident", [128, 128], F32)
        sones = sb("sones", [128, 1], BF16)
        sonef = sb("sonef", [1, 128], F32)
        scarry = sb("scarry", [128, NT * S], F32)
        sxpa = sb("sxpa", [128, SEG], F32)
        sxpb = sb("sxpb", [48, SEG], F32)
        sxda = sxpa
        sxdb = sxpb
        sdtr = sb("sdtr", [128, SEG], BF16)
        st_mu = sb("st_mu", [1, SEG], F32)
        st_e2 = sb("st_e2", [1, SEG], F32)
        st_t1 = sb("st_t1", [1, SEG], F32)
        st_a = sb("st_a", [1, SEG], F32)
        st_b = sb("st_b", [1, SEG], F32)
        sab_a = sb("sab_a", [128, SEG], F32)
        sab_b = sb("sab_b", [128, SEG], F32)
        sct = sb("sct", [128, 2 * SEG], F32)
        stmp = sb("stmp", [128, 2 * SEG], F32)
        szs = sb("szs", [128, 2 * SEG], BF16)
        stmp2 = sct
        sevac = szs


        swin = arA[:]
        sh = arA[:, 0:S * SEG]
        sdA = [arA[:, S * SEG:2 * S * SEG],
               arA[:, 2 * S * SEG:3 * S * SEG]]
        sgx = [arA[:, 0:SEG], arA[:, SEG:2 * SEG]]
        sux = [arA[:, 2 * SEG:3 * SEG], arA[:, 3 * SEG:4 * SEG]]
        sqx = [arA[:, 4 * SEG:5 * SEG], arA[:, 5 * SEG:6 * SEG]]
        srx = [arA[:, 6 * SEG:7 * SEG], arA[:, 7 * SEG:8 * SEG]]
        sinter = arA[:, 8 * SEG:(8 + NT) * SEG]
        sw_t = arA[:, 3 * S * SEG:3 * S * SEG + SEG]
        sdbu = arA[:, 3 * S * SEG + SEG:3 * S * SEG + 2 * SEG]

        arBb = arB[:].bitcast(BF16)
        sBbc = arBb[:, 0:S * SEG]
        sCbc = arBb[:, S * SEG:2 * S * SEG]
        sxf = arB[:]
        sacc = arB[:]
        arBi = arB[:].bitcast(I16)
        sdk = arBi[:, 0:1024]
        sdn = [arBi[:, 1024:2048], arBi[:, 2048:3072]]
        sdm = arBi[:, 3072:4096]
        sdq = arBi[:, 4096:5120]
        sdz = arBi[:, 5120:6144]
        sdsc = arB[:, 3072:4096]
        sdv = [arBb[:, 8192:9216], arBb[:, 9216:10240]]
        smine = sdt[:]
        sresb = sdt[:]
        swd = sdt[:]
        swg = [sxs[:, 0:NT * 128], sxs[:, NT * 128:2 * NT * 128]]
        swu = [sxs[:, 2 * NT * 128:3 * NT * 128], sxs[:, 3 * NT * 128:4 * NT * 128]]
        sh2t = sxn[:]
        sout = su[:].bitcast(F32)

        pb = [es.enter_context(nc.psum_tensor(f"pb{i}", [128, 512], F32))
              for i in range(8)]

        qS = Sem(nc, es, "qS")
        qG = Sem(nc, es, "qG")
        sPE = Sem(nc, es, "sPE")
        sACT = Sem(nc, es, "sACT")
        sDVE = Sem(nc, es, "sDVE")
        sST = Sem(nc, es, "sST")
        sCCg = Sem(nc, es, "sCCg")
        sCCu = Sem(nc, es, "sCCu")
        sCCd = Sem(nc, es, "sCCd")
        sCCx = [Sem(nc, es, f"sCCx{i}") for i in range(NSEG)]
        sCCw = [Sem(nc, es, f"sCCw{i}") for i in range(NSEG)]

        seps = sb("seps", [128, 1], F32)
        sone_b = sb("sone_b", [128, 1], F32)

        blk = es.enter_context(nc.Block())

        bank_ready = [(sACT, 0) for _ in range(8)]

        def wait_bank(t, b):
            sem, val = bank_ready[b]
            if val > 0:
                t.wait_ge(sem.h, val)

        def seg_sl(s):
            return slice(SEG * s, SEG * (s + 1))

        def dm3(dram_ap, w):
            return dram_ap.rearrange("(i p) t -> p i t", p=128)

        def sb3(sbuf_ap, w):
            return sbuf_ap.rearrange("p (i t) -> p i t", t=w)

        @blk.sync
        def _(sync):
            for src, dst in [(convw, sconvw), (convb, sconvb), (bdt, sbdt),
                             (dvec, sdvec), (alog, sA), (ident, sident)]:
                sync.dma_start(out=dst[:], in_=src[:]).then_inc(qS.h, 16)
                qS.val += 16

        @blk.gpsimd
        def _(g):
            g.dma_start(out=sb3(swx[:], 176), in_=dm3(wxT[:], 176)
                        ).then_inc(qG.h, 16)
            g.dma_start(out=swdt[:], in_=wdtT[:]).then_inc(qG.h, 16)
            qG.val += 32

        @blk.vector
        def _(v):
            v.memset(sones[:], 1.0)
            v.memset(sonef[:], 1.0)
            v.memset(seps[:], EPS)
            v.memset(sone_b[:], 1.0)
            v.wait_ge(qS.h, qS.val)
            v.tensor_scalar(sconvbn[:], sconvb[:], -1.0, None, ALU.mult)
            v.memset(scarry[:], 0.0)
            hdr = sxs[:].rearrange("p (i w) -> p i w", w=SEG + 3)
            sDVE.inc(v.memset(hdr[:, :, 0:3], 0.0))
        init_dve = sDVE.val

        @blk.scalar
        def _(sc):
            sc.wait_ge(qS.h, qS.val)
            sc.wait_ge(sDVE.h, init_dve)
            sc.activation(sA[:], sA[:], AF.Exp)
            sACT.inc(sc.activation(sA[:], sA[:], AF.Copy, scale=-1.0))

        dec_dve = [0]
        for mat, (pk_h, sc_h, agin) in (("g", (gpk, gsc, ag_g_in)),
                                        ("u", (upk, usc, ag_u_in)),
                                        ("d", (dpk, dsc, ag_d_in))):
            for t8 in range(8):
                for ch in range(2):
                    csl = slice(1024 * ch, 1024 * (ch + 1))

                    @blk.gpsimd
                    def _(g, t8=t8, csl=csl):
                        if dec_dve[0] > 0:
                            g.wait_ge(sDVE.h, dec_dve[0])
                        g.dma_start(out=sdk[:], in_=pk_h[128 * t8:128 * (t8 + 1), csl]
                                    ).then_inc(qG.h, 16)
                        g.dma_start(out=sdsc[:],
                                    in_=sc_h[128 * t8:128 * (t8 + 1), csl]
                                    ).then_inc(qG.h, 16)
                        qG.val += 32

                    @blk.vector
                    def _(v, t8=t8):
                        v.wait_ge(qG.h, qG.val)
                        v.tensor_scalar(sdn[0], sdk[:], 15, None, ALU.bitwise_and)
                        v.tensor_scalar(sdn[1], sdk[:], 4, None,
                                        ALU.logical_shift_right)
                        for nb in range(2):
                            c = sdn[nb]
                            v.tensor_scalar(sdm[:], c, 7, None, ALU.bitwise_and)
                            v.tensor_scalar(sdz[:], sdm[:], 2, None, ALU.is_ge)
                            v.tensor_tensor(sdq[:], sdm[:], sdz[:], ALU.add)
                            v.tensor_scalar(sdq[:], sdq[:], 0x40, 0x3EC0,
                                            ALU.mult, ALU.add)
                            v.tensor_scalar(sdz[:], sdm[:], 1, None, ALU.is_ge)
                            v.tensor_tensor(sdq[:], sdq[:], sdz[:], ALU.mult)
                            v.tensor_scalar(sdz[:], c, 8, 12, ALU.bitwise_and,
                                            ALU.logical_shift_left)
                            v.tensor_tensor(sdq[:], sdq[:], sdz[:], ALU.add)
                            last = v.tensor_tensor(sdv[nb], sdq[:].bitcast(BF16),
                                                   sdsc[:], ALU.mult)
                            if nb == 1:
                                sDVE.inc(last)
                        dec_dve[0] = sDVE.val

                    @blk.gpsimd
                    def _(g, t8=t8, ch=ch, mat=mat):
                        g.wait_ge(sDVE.h, dec_dve[0])
                        for nb in range(2):
                            if mat != "d":
                                kt = 2 * t8 + nb
                                dst = agin.rearrange("(a p) (i n) -> p a i n",
                                                     p=128, n=128)
                                dst = dst[:, 8 * ch:8 * (ch + 1), kt, :]
                                src = sdv[nb].rearrange("p (a n) -> p a n", n=128)
                            else:
                                r0 = 256 * t8 + 128 * nb
                                dst = agin[r0:r0 + 128, 1024 * ch:1024 * (ch + 1)]
                                src = sdv[nb]
                            g.dma_start(out=dst, in_=src).then_inc(qG.h, 16)
                            qG.val += 16

        @blk.gpsimd
        def _(g):
            g.wait_ge(qG.h, qG.val)
            sCCg.inc(g.collective_compute(
                "AllGather", ALU.bypass, replica_groups=RG8,
                ins=[ag_g_in[:].opt()], outs=[WgD[:].opt()]))
            sCCu.inc(g.collective_compute(
                "AllGather", ALU.bypass, replica_groups=RG8,
                ins=[ag_u_in[:].opt()], outs=[WuD[:].opt()]))
            sCCd.inc(g.collective_compute(
                "AllGather", ALU.bypass, replica_groups=RG8,
                ins=[ag_d_in[:].opt()], outs=[WdD[:].opt()]))

        dec_done_dve = sDVE.val
        dec_done_qg = qG.val

        xn_ready = [0] * NSEG
        u_ready = [0] * NSEG
        hdr_ready = [0] * NSEG
        bc_done = [0] * NSEG
        dt_ready = [0] * NSEG
        scan_done = [0] * NSEG
        p_done = [0] * NSEG
        res_done = [0] * NSEG
        h2_done = [0] * NSEG
        wout_mm_done = [0] * NSEG
        zmm_done = [0] * NSEG
        xdmm_done = [0] * NSEG
        agin_dma = [0] * NSEG
        xdsum_done = [0] * NSEG
        dtmm_done = [0] * NSEG
        dAlast = [0] * NSEG

        def emit_ln(tag, src_bf, sq_buf, dst_bf, s):
            @blk.scalar
            def _(sc):
                sACT.inc(sc.activation(sq_buf, src_bf, AF.Square))
            sqv = sACT.val

            @blk.tensor
            def _(t):
                t.wait_ge(sACT.h, sqv)
                wait_bank(t, 6)
                wait_bank(t, 7)
                for i in range(NT):
                    t.matmul(pb[6][0:1, :], sones[:], src_bf[:, 512 * i:512 * (i + 1)],
                             start=(i == 0), stop=(i == NT - 1))
                for i in range(NT):
                    m = t.matmul(pb[7][0:1, :], sones[:],
                                 sq_buf[:, 512 * i:512 * (i + 1)],
                                 start=(i == 0), stop=(i == NT - 1))
                sPE.inc(m)
            stv = sPE.val

            @blk.vector
            def _(v):
                v.wait_ge(sPE.h, stv)
                v.tensor_scalar(st_mu[:], pb[6][0:1, :], 1.0 / D, None, ALU.mult)
                v.tensor_scalar(st_e2[:], pb[7][0:1, :], 1.0 / D, None, ALU.mult)
                v.tensor_tensor(st_t1[:], st_mu[:], st_mu[:], ALU.mult)
                sDVE.inc(v.tensor_tensor(st_t1[:], st_e2[:], st_t1[:], ALU.subtract))
            varv = sDVE.val

            @blk.scalar
            def _(sc):
                sc.wait_ge(sDVE.h, varv)
                sc.activation(st_t1[:], st_t1[:], AF.Ln, bias=seps[0:1, 0:1])
                sACT.inc(sc.activation(st_a[:], st_t1[:], AF.Exp, scale=-0.5))
            rsv = sACT.val

            @blk.vector
            def _(v):
                v.wait_ge(sACT.h, rsv)
                v.tensor_tensor(st_b[:], st_mu[:], st_a[:], ALU.mult)
                sDVE.inc(v.tensor_scalar(st_b[:], st_b[:], -1.0, None, ALU.mult))
            abv = sDVE.val

            @blk.tensor
            def _(t):
                t.wait_ge(sDVE.h, abv)
                t.matmul(pb[6][:, :], sonef[:], st_a[:], start=True, stop=True)
                m = t.matmul(pb[7][:, :], sonef[:], st_b[:], start=True, stop=True)
                sPE.inc(m)
            bcv = sPE.val

            @blk.scalar
            def _(sc):
                sc.wait_ge(sPE.h, bcv)
                sc.copy(sab_a[:], pb[6][:, :])
                sACT.inc(sc.copy(sab_b[:], pb[7][:, :]))
            bank_ready[6] = (sACT, sACT.val)
            bank_ready[7] = (sACT, sACT.val)
            abv2 = sACT.val

            @blk.vector
            def _(v):
                v.wait_ge(sACT.h, abv2)
                last = None
                for i in range(NT):
                    dsl = slice(SEG * i, SEG * (i + 1))
                    v.tensor_tensor(dst_bf[:, dsl], src_bf[:, dsl], sab_a[:],
                                    ALU.mult)
                    last = v.tensor_tensor(dst_bf[:, dsl], dst_bf[:, dsl],
                                           sab_b[:], ALU.add)
                sDVE.inc(last)
            return sDVE.val

        for s in range(NSEG):
            @blk.sync
            def _(sync, s=s):
                if s > 0:
                    sync.wait_ge(qG.h, h2_done[s - 1])
                else:
                    sync.wait_ge(sDVE.h, dec_done_dve)
                    sync.wait_ge(qG.h, dec_done_qg)
                src = dm3(xT[:, seg_sl(s)], SEG)
                sync.dma_start(out=sb3(sxf[:, 0:NT * SEG], SEG),
                               in_=src).then_inc(qS.h, 16)
                qS.val += 16
            xload_v = qS.val

            @blk.gpsimd
            def _(g, s=s):
                if s > 0:
                    g.wait_ge(sPE.h, wout_mm_done[s - 1])
                for i in range(NT):
                    g.dma_start(out=swin[:, 2048 * i:2048 * (i + 1)],
                                in_=winT[128 * i:128 * (i + 1), 0:2048]
                                ).then_inc(qG.h, 16)
                    qG.val += 16
            wld_f = qG.val

            @blk.vector
            def _(v, s=s):
                v.wait_ge(qS.h, xload_v)
                if s > 0:
                    v.wait_ge(sPE.h, zmm_done[s - 1])
                    v.wait_ge(qG.h, h2_done[s - 1])
                sDVE.inc(v.tensor_copy(sxn[:], sxf[:, 0:NT * SEG]))
            @blk.scalar
            def _(sc, s=s):
                sc.wait_ge(sDVE.h, sDVE.val)
                if s > 0:
                    sc.wait_ge(sPE.h, wout_mm_done[s - 1])
            xn_ready[s] = emit_ln(f"ln1_{s}", sxn[:], su[:], sxn[:], s)

            for j in range(NT):
                bj = j % 2
                @blk.tensor
                def _(t, s=s, j=j, bj=bj):
                    if j == 0:
                        t.wait_ge(sDVE.h, xn_ready[s])
                        t.wait_ge(qG.h, wld_f)
                        if s > 0:
                            t.wait_ge(sDVE.h, hdr_ready[s - 1])
                    wait_bank(t, bj)
                    for i in range(NT):
                        m = t.matmul(pb[bj][:, :],
                                     swin[:, 2048 * i + 128 * j:2048 * i + 128 * (j + 1)],
                                     sxn[:, 512 * i:512 * (i + 1)],
                                     start=(i == 0), stop=(i == NT - 1))
                    sPE.inc(m)
                mmv = sPE.val

                @blk.scalar
                def _(sc, s=s, j=j, bj=bj):
                    sc.wait_ge(sPE.h, mmv)
                    sACT.inc(sc.copy(sxs[:, (SEG + 3) * j + 3:(SEG + 3) * (j + 1)],
                                     pb[bj][:, :]))
                bank_ready[bj] = (sACT, sACT.val)
                evv = sACT.val

                @blk.vector
                def _(v, s=s, j=j):
                    v.wait_ge(sACT.h, evv)
                    base = (SEG + 3) * j
                    cp = j % 2
                    v.tensor_scalar(sct[:, SEG * cp:SEG * (cp + 1)],
                                    sxs[:, base:base + SEG],
                                    sconvw[:, DC * j:DC * j + 1], None, ALU.mult)
                    for k in range(1, DC):
                        v.scalar_tensor_tensor(
                            sct[:, SEG * cp:SEG * (cp + 1)],
                            sxs[:, base + k:base + k + SEG],
                            sconvw[:, DC * j + k:DC * j + k + 1],
                            sct[:, SEG * cp:SEG * (cp + 1)], ALU.mult, ALU.add)
                    sDVE.inc(v.tensor_copy(sct[0:1, SEG * cp:SEG * cp + 1],
                                           sct[0:1, SEG * cp:SEG * cp + 1]))
                cvv = sDVE.val

                @blk.scalar
                def _(sc, s=s, j=j):
                    sc.wait_ge(sDVE.h, cvv)
                    cp = j % 2
                    csl = slice(SEG * cp, SEG * (cp + 1))
                    sc.activation(stmp[:, csl], sct[:, csl], AF.Exp, scale=-1.0,
                                  bias=sconvbn[:, j:j + 1])
                    sc.activation(stmp[:, csl], stmp[:, csl],
                                  AF.Ln, bias=sone_b[:, 0:1])
                    sACT.inc(sc.activation(stmp[:, csl], stmp[:, csl],
                                           AF.Exp, scale=-1.0))
                sgv = sACT.val

                @blk.vector
                def _(v, s=s, j=j):
                    v.wait_ge(sACT.h, sgv)
                    cp = j % 2
                    csl = slice(SEG * cp, SEG * (cp + 1))
                    sDVE.inc(v.scalar_tensor_tensor(
                        su[:, SEG * j:SEG * (j + 1)], sct[:, csl],
                        sconvb[:, j:j + 1],
                        stmp[:, csl],
                        ALU.add, ALU.mult))
            u_ready[s] = sDVE.val

            @blk.vector
            def _(v, s=s):
                x3s = sxs[:].rearrange("p (i w) -> p i w", w=SEG + 3)
                sDVE.inc(v.tensor_copy(x3s[:, :, 0:3], x3s[:, :, SEG:SEG + 3]))
            hdr_ready[s] = sDVE.val

            @blk.tensor
            def _(t, s=s):
                t.wait_ge(sDVE.h, u_ready[s])
                wait_bank(t, 2)
                wait_bank(t, 3)
                for i in range(NT):
                    t.matmul(pb[2][:, :], swx[:, 176 * i:176 * i + 128],
                             su[:, 512 * i:512 * (i + 1)],
                             start=(i == 0), stop=(i == NT - 1))
                for i in range(NT):
                    m = t.matmul(pb[3][0:48, :], swx[:, 176 * i + 128:176 * (i + 1)],
                                 su[:, 512 * i:512 * (i + 1)],
                                 start=(i == 0), stop=(i == NT - 1))
                sPE.inc(m)
            xdmm_done[s] = sPE.val

            @blk.scalar
            def _(sc, s=s):
                sc.wait_ge(sPE.h, xdmm_done[s])
                if s > 0:
                    sc.wait_ge(qG.h, bc_done[s - 1])
                sc.copy(sxpa[:], pb[2][:, :])
                sACT.inc(sc.copy(sxpb[:], pb[3][0:48, :]))
            bank_ready[2] = (sACT, sACT.val)
            bank_ready[3] = (sACT, sACT.val)
            xdev = sACT.val

            @blk.gpsimd
            def _(g, s=s):
                g.wait_ge(sACT.h, xdev)
                g.dma_start(out=ag_xd_in[s][0:128, :], in_=sxpa[:]).then_inc(qG.h, 16)
                g.dma_start(out=ag_xd_in[s][128:176, :], in_=sxpb[:]).then_inc(qG.h, 16)
                qG.val += 32
                agin_dma[s] = qG.val
                g.wait_ge(qG.h, qG.val)
                sCCx[s].inc(g.collective_compute(
                    "AllReduce", ALU.add, replica_groups=RGP,
                    ins=[ag_xd_in[s][:].opt()], outs=[ag_xd_out[s][:].opt()]))

            @blk.gpsimd
            def _(g, s=s):
                g.wait_ge(sCCx[s].h, 1)
                if s > 0:
                    g.wait_ge(sPE.h, dtmm_done[s - 1])
                    g.wait_ge(sDVE.h, res_done[s - 1])
                    g.wait_ge(qG.h, h2_done[s - 1])
                g.dma_start(out=sdtr[:], in_=ag_xd_out[s][0:128, :]
                            ).then_inc(qG.h, 16)
                srcB = ag_xd_out[s][128:144, :].unsqueeze(0).broadcast_to(
                    [128, 16, SEG])
                g.dma_start(out=sBbc.rearrange("p (g t) -> p g t", t=SEG),
                            in_=srcB).then_inc(qG.h, 16)
                srcC = ag_xd_out[s][160:176, :].unsqueeze(0).broadcast_to(
                    [128, 16, SEG])
                g.dma_start(out=sCbc.rearrange("p (g t) -> p g t", t=SEG),
                            in_=srcC).then_inc(qG.h, 16)
                qG.val += 48
            bc_done[s] = qG.val
            xdsum_done[s] = 0

            for j in range(NT):
                bj = 4 + (j % 2)
                jp = j % 2
                @blk.tensor
                def _(t, s=s, j=j, bj=bj):
                    if j == 0:
                        t.wait_ge(qG.h, bc_done[s])
                    wait_bank(t, bj)
                    m = t.matmul(pb[bj][:, :], swdt[:, 128 * j:128 * (j + 1)],
                                 sdtr[:], start=True, stop=True)
                    sPE.inc(m)
                mmv = sPE.val

                @blk.scalar
                def _(sc, s=s, j=j, bj=bj, jp=jp):
                    sc.wait_ge(sPE.h, mmv)
                    if j >= 2:
                        sc.wait_ge(sDVE.h, dt_tt[jp])
                    tsl = slice(SEG * jp, SEG * (jp + 1))
                    sc.activation(stmp[:, tsl], pb[bj][:, :], AF.Abs,
                                  bias=sbdt[:, j:j + 1])
                    sc.activation(stmp[:, tsl], stmp[:, tsl], AF.Exp, scale=-1.0)
                    sc.activation(stmp[:, tsl], stmp[:, tsl], AF.Ln, bias=sone_b[:, 0:1])
                    sACT.inc(sc.activation(stmp2[:, tsl], pb[bj][:, :], AF.Relu,
                                           bias=sbdt[:, j:j + 1]))
                spv = sACT.val
                bank_ready[bj] = (sACT, sACT.val)

                @blk.vector
                def _(v, s=s, j=j, jp=jp):
                    v.wait_ge(sACT.h, spv)
                    tsl = slice(SEG * jp, SEG * (jp + 1))
                    sDVE.inc(v.tensor_tensor(sdt[:, SEG * j:SEG * (j + 1)],
                                             stmp[:, tsl], stmp2[:, tsl], ALU.add))
                if j == 0:
                    dt_tt = [0, 0]
                dt_tt[jp] = sDVE.val
            dt_ready[s] = sDVE.val
            dtmm_done[s] = sPE.val

            dA_wr = [0, 0]
            dA_rd = [0, 0]
            for j in range(NT):
                pp = j % 2
                @blk.scalar
                def _(sc, s=s, j=j, pp=pp):
                    if j == 0:
                        sc.wait_ge(sDVE.h, dt_ready[s])
                    if dA_rd[pp] > 0:
                        sc.wait_ge(sDVE.h, dA_rd[pp])
                    last = None
                    for sig in range(S):
                        last = sc.activation(
                            sdA[pp][:, SEG * sig:SEG * (sig + 1)],
                            sdt[:, SEG * j:SEG * (j + 1)], AF.Exp,
                            scale=sA[:, S * j + sig:S * j + sig + 1])
                    sACT.inc(last)
                dA_wr[pp] = sACT.val

                @blk.vector
                def _(v, s=s, j=j, pp=pp):
                    if j == 0:
                        v.wait_ge(qG.h, bc_done[s])
                    v.wait_ge(sACT.h, dA_wr[pp])
                    v.tensor_tensor(sw_t[:], sdt[:, SEG * j:SEG * (j + 1)],
                                    su[:, SEG * j:SEG * (j + 1)], ALU.mult)
                    for sig in range(S):
                        ssl = slice(SEG * sig, SEG * (sig + 1))
                        v.tensor_tensor(sdbu[:], sw_t[:], sBbc[:, ssl], ALU.mult)
                        init = 0.0 if s == 0 else \
                            scarry[:, S * j + sig:S * j + sig + 1]
                        v.tensor_tensor_scan(sh[:, ssl], sdA[pp][:, ssl], sdbu[:],
                                             init, ALU.mult, ALU.add)
                    hh = sh.rearrange("p (g t) -> p g t", t=SEG)
                    v.tensor_copy(scarry[:, S * j:S * (j + 1)], hh[:, :, SEG - 1])
                    v.tensor_tensor(sdA[pp], sh, sCbc, ALU.mult)
                    half = S * SEG // 2
                    while half >= SEG:
                        v.tensor_tensor(sdA[pp][:, 0:half], sdA[pp][:, 0:half],
                                        sdA[pp][:, half:2 * half], ALU.add)
                        half //= 2
                    last = v.scalar_tensor_tensor(
                        su[:, SEG * j:SEG * (j + 1)], su[:, SEG * j:SEG * (j + 1)],
                        sdvec[:, j:j + 1], sdA[pp][:, 0:SEG], ALU.mult, ALU.add)
                    sDVE.inc(last)
                    dA_rd[pp] = sDVE.val
            scan_done[s] = sDVE.val
            dAlast[s] = sACT.val

            @blk.gpsimd
            def _(g, s=s):
                g.wait_ge(sPE.h, xdmm_done[s])
                g.wait_ge(sDVE.h, scan_done[s])
                for i in range(NT):
                    g.dma_start(out=swin[:, 2048 * i:2048 * (i + 1)],
                                in_=winT[128 * i:128 * (i + 1), 2048:4096]
                                ).then_inc(qG.h, 16)
                    qG.val += 16
            zw_v = qG.val

            for j in range(NT):
                bj = j % 2
                jp = j % 2
                @blk.tensor
                def _(t, s=s, j=j, bj=bj):
                    if j == 0:
                        t.wait_ge(qG.h, zw_v)
                    wait_bank(t, bj)
                    for i in range(NT):
                        m = t.matmul(pb[bj][:, :],
                                     swin[:, 2048 * i + 128 * j:2048 * i + 128 * (j + 1)],
                                     sxn[:, 512 * i:512 * (i + 1)],
                                     start=(i == 0), stop=(i == NT - 1))
                    sPE.inc(m)
                mmv = sPE.val

                @blk.scalar
                def _(sc, s=s, j=j, bj=bj, jp=jp):
                    sc.wait_ge(sPE.h, mmv)
                    if j >= 2:
                        sc.wait_ge(sDVE.h, p_tt[jp])
                    csl = slice(SEG * jp, SEG * (jp + 1))
                    sc.activation(stmp[:, csl], pb[bj][:, :], AF.Exp, scale=-1.0)
                    sc.activation(stmp[:, csl], stmp[:, csl],
                                  AF.Ln, bias=sone_b[:, 0:1])
                    sACT.inc(sc.activation(stmp[:, csl], stmp[:, csl],
                                           AF.Exp, scale=-1.0))
                zev = sACT.val

                @blk.vector
                def _(v, s=s, j=j, jp=jp, bj=bj):
                    v.wait_ge(sACT.h, zev)
                    if j == 0:
                        v.wait_ge(sDVE.h, scan_done[s])
                    csl = slice(SEG * jp, SEG * (jp + 1))
                    v.tensor_tensor(szs[:, csl], pb[bj][:, :],
                                    stmp[:, csl], ALU.mult)
                    sDVE.inc(v.tensor_tensor(su[:, SEG * j:SEG * (j + 1)],
                                             su[:, SEG * j:SEG * (j + 1)],
                                             szs[:, csl],
                                             ALU.mult))
                if j == 0:
                    p_tt = [0, 0]
                p_tt[jp] = sDVE.val
                bank_ready[bj] = (sDVE, sDVE.val)
            p_done[s] = sDVE.val
            zmm_done[s] = sPE.val

            @blk.gpsimd
            def _(g, s=s):
                g.wait_ge(sPE.h, zmm_done[s])
                for i in range(NT):
                    g.dma_start(out=swin[:, 2048 * i:2048 * (i + 1)],
                                in_=woutT[128 * i:128 * (i + 1), :]
                                ).then_inc(qG.h, 16)
                    qG.val += 16
            wow_v = qG.val

            ag_wo_dma = 0
            for mt in range(NT):
                bj = 2 + (mt % 2)
                mp = mt % 2
                @blk.tensor
                def _(t, s=s, mt=mt, bj=bj):
                    if mt == 0:
                        t.wait_ge(qG.h, wow_v)
                        t.wait_ge(sDVE.h, p_done[s])
                    wait_bank(t, bj)
                    for i in range(NT):
                        m = t.matmul(pb[bj][:, :],
                                     swin[:, 2048 * i + 128 * mt:2048 * i + 128 * (mt + 1)],
                                     su[:, 512 * i:512 * (i + 1)],
                                     start=(i == 0), stop=(i == NT - 1))
                    sPE.inc(m)
                mmv = sPE.val

                @blk.scalar
                def _(sc, s=s, mt=mt, bj=bj, mp=mp):
                    sc.wait_ge(sPE.h, mmv)
                    if mt >= 2:
                        sc.wait_ge(qG.h, wo_dma[mp])
                    sACT.inc(sc.copy(sevac[:, SEG * mp:SEG * (mp + 1)],
                                     pb[bj][:, :]))
                bank_ready[bj] = (sACT, sACT.val)
                wev = sACT.val

                @blk.gpsimd
                def _(g, s=s, mt=mt, mp=mp):
                    g.wait_ge(sACT.h, wev)
                    g.dma_start(out=ag_wo_in[s][128 * mt:128 * (mt + 1), :],
                                in_=sevac[:, SEG * mp:SEG * (mp + 1)]
                                ).then_inc(qG.h, 16)
                    qG.val += 16
                if mt == 0:
                    wo_dma = [0, 0]
                wo_dma[mp] = qG.val
            wout_mm_done[s] = sPE.val

            @blk.gpsimd
            def _(g, s=s):
                g.wait_ge(qG.h, qG.val)
                sCCw[s].inc(g.collective_compute(
                    "AllReduce", ALU.add, replica_groups=RGP,
                    ins=[ag_wo_in[s][:].opt()], outs=[ag_wo_out[s][:].opt()]))

            @blk.sync
            def _(sync, s=s):
                sync.wait_ge(sCCw[s].h, 1)
                sync.wait_ge(sACT.h, dAlast[s])
                sync.dma_start(out=sb3(smine[:], SEG),
                               in_=dm3(ag_wo_out[s][:], SEG)).then_inc(qS.h, 16)
                src3 = dm3(xT[:, seg_sl(s)], SEG)
                sync.dma_start(out=sb3(sxf[:, 0:NT * SEG], SEG),
                               in_=src3).then_inc(qS.h, 16)
                qS.val += 32
            agw_read = qS.val

            @blk.vector
            def _(v, s=s):
                v.wait_ge(qS.h, agw_read)
                sDVE.inc(v.tensor_tensor(sxf[:, 0:NT * SEG], sxf[:, 0:NT * SEG],
                                         smine[:], ALU.add))
            res_done[s] = sDVE.val

            @blk.vector
            def _(v, s=s):
                sDVE.inc(v.tensor_copy(sresb[:], sxf[:, 0:NT * SEG]))
            resb_v = sDVE.val

            @blk.scalar
            def _(sc, s=s):
                sc.wait_ge(sDVE.h, resb_v)
            h2v = emit_ln(f"ln2_{s}", sresb[:], sxn[:], sxn[:], s)

            @blk.gpsimd
            def _(g, s=s):
                g.wait_ge(sDVE.h, h2v)
                dst1 = dm3(resid_d[:, seg_sl(s)], SEG)
                g.dma_start(out=dst1, in_=sb3(sxf[:, 0:NT * SEG], SEG)
                            ).then_inc(qG.h, 16)
                dst2 = dm3(h2_d[:, seg_sl(s)], SEG)
                g.dma_start(out=dst2, in_=sb3(sxn[:], SEG)).then_inc(qG.h, 16)
                qG.val += 32
            h2_done[s] = qG.val

        @blk.sync
        def _(sync):
            sync.wait_ge(sCCg.h, 1)
            sync.wait_ge(sCCu.h, 1)
            sync.wait_ge(sCCd.h, 1)

        gu_mm_prev_pass = 0
        gu_rd = [0, 0]
        swd_use = [0]
        st_prev_pass = 0
        st_prev_dve = 0
        ffn_gate_dve = sDVE.val
        ffn_gate_act = sACT.val
        ffn_gate_pe = sPE.val
        for p in range(NPASS):
            @blk.sync
            def _(sync, p=p):
                sync.wait_ge(qG.h, h2_done[NSEG - 1])
                if p == 0:
                    sync.wait_ge(sDVE.h, ffn_gate_dve)
                    sync.wait_ge(sACT.h, ffn_gate_act)
                    sync.wait_ge(sPE.h, ffn_gate_pe)
                else:
                    sync.wait_ge(sPE.h, gu_mm_prev_pass)
                    sync.wait_ge(sDVE.h, st_prev_dve)
                pid = nc.partition_id([ET.SP])
                for r in sync.Switch(pid, 8):
                    sseg = 2 * p + (int(r) % 2)
                    src = dm3(h2_d[:, SEG * sseg:SEG * (sseg + 1)], SEG)
                    sync.dma_start(out=sb3(sh2t[:], SEG), in_=src
                                   ).then_inc(qS.h, 16)
                    src2 = dm3(resid_d[:, SEG * sseg:SEG * (sseg + 1)], SEG)
                    sync.dma_start(out=sb3(sacc[:, 0:NT * SEG], SEG), in_=src2
                                   ).then_inc(qS.h, 16)
                qS.val += 32
            h2t_v = qS.val

            dmm_last_sb = 0
            wg_use = [0, 0]
            for sbi in range(NSB):
                inter_last = 0
                for j in range(NT):
                    ntile = NT * sbi + j
                    pp2 = ntile % 2

                    @blk.gpsimd
                    def _(g, p=p, ntile=ntile, pp2=pp2):
                        if wg_use[pp2] > 0:
                            g.wait_ge(sPE.h, wg_use[pp2])
                        g.dma_start(out=swg[pp2],
                                    in_=WgD[128 * ntile:128 * (ntile + 1), :]
                                    ).then_inc(qG.h, 16)
                        g.dma_start(out=swu[pp2],
                                    in_=WuD[128 * ntile:128 * (ntile + 1), :]
                                    ).then_inc(qG.h, 16)
                        qG.val += 32
                    wld_v = qG.val

                    bg, bu = 0 + pp2, 2 + pp2
                    @blk.tensor
                    def _(t, p=p, pp2=pp2, bg=bg, bu=bu, wld_v=wld_v):
                        t.wait_ge(qG.h, wld_v)
                        t.wait_ge(qS.h, h2t_v)
                        wait_bank(t, bg)
                        wait_bank(t, bu)
                        for i in range(NT):
                            t.matmul(pb[bg][:, :], swg[pp2][:, 128 * i:128 * (i + 1)],
                                     sh2t[:, 512 * i:512 * (i + 1)],
                                     start=(i == 0), stop=(i == NT - 1))
                        for i in range(NT):
                            m = t.matmul(pb[bu][:, :],
                                         swu[pp2][:, 128 * i:128 * (i + 1)],
                                         sh2t[:, 512 * i:512 * (i + 1)],
                                         start=(i == 0), stop=(i == NT - 1))
                        sPE.inc(m)
                    wg_use[pp2] = sPE.val
                    gumm_v = sPE.val

                    @blk.scalar
                    def _(sc, j=j, bg=bg, bu=bu, pp2=pp2):
                        sc.wait_ge(sPE.h, gumm_v)
                        if gu_rd[pp2] > 0:
                            sc.wait_ge(sDVE.h, gu_rd[pp2])
                        sc.copy(sgx[pp2], pb[bg][:, :])
                        sc.activation(sqx[pp2], pb[bg][:, :], AF.Square)
                        sACT.inc(sc.copy(sux[pp2], pb[bu][:, :]))
                    guev_v = sACT.val
                    bank_ready[bg] = (sACT, sACT.val)
                    bank_ready[bu] = (sACT, sACT.val)

                    @blk.vector
                    def _(v, j=j, pp2=pp2, sbi=sbi):
                        v.wait_ge(sACT.h, guev_v)
                        if sbi > 0 and j == 0:
                            v.wait_ge(sPE.h, dmm_last_sb)
                        v.tensor_scalar(srx[pp2], sqx[pp2], GB, 1.0,
                                        ALU.mult, ALU.add)
                        sDVE.inc(v.tensor_tensor(srx[pp2], sgx[pp2], srx[pp2],
                                                 ALU.mult))
                    p3_v = sDVE.val

                    @blk.scalar
                    def _(sc, pp2=pp2):
                        sc.wait_ge(sDVE.h, p3_v)
                        sACT.inc(sc.activation(srx[pp2], srx[pp2], AF.Tanh,
                                               scale=GA))
                    th_v = sACT.val

                    @blk.vector
                    def _(v, j=j, pp2=pp2):
                        v.wait_ge(sACT.h, th_v)
                        v.tensor_scalar(srx[pp2], srx[pp2], 0.5, 0.5,
                                        ALU.mult, ALU.add)
                        v.tensor_tensor(sgx[pp2], sgx[pp2], sux[pp2], ALU.mult)
                        last = v.tensor_tensor(sinter[:, SEG * j:SEG * (j + 1)],
                                               sgx[pp2], srx[pp2], ALU.mult)
                        sDVE.inc(last)
                    if sbi == 0 and j == 0 and p == 0:
                        pass
                    gu_rd[pp2] = sDVE.val
                    inter_last = sDVE.val

                for mg in range(4):
                    @blk.gpsimd
                    def _(g, p=p, sbi=sbi, mg=mg):
                        if swd_use[0] > 0:
                            g.wait_ge(sPE.h, swd_use[0])
                        for j in range(NT):
                            nt2 = NT * sbi + j
                            g.dma_start(
                                out=swd[:, SEG * j:SEG * (j + 1)],
                                in_=WdD[128 * nt2:128 * (nt2 + 1),
                                        512 * mg:512 * (mg + 1)]
                                ).then_inc(qG.h, 16)
                        qG.val += 16 * NT
                    dwl_v = qG.val

                    for mt in range(4):
                        bd = 4 + (mt % 2)
                        mtg = 4 * mg + mt
                        @blk.tensor
                        def _(t, mg=mg, mt=mt, bd=bd, dwl_v=dwl_v,
                              inter_last=inter_last):
                            if mt == 0:
                                t.wait_ge(qG.h, dwl_v)
                                t.wait_ge(sDVE.h, inter_last)
                            wait_bank(t, bd)
                            for j in range(NT):
                                m = t.matmul(
                                    pb[bd][:, :],
                                    swd[:, SEG * j + 128 * mt:SEG * j + 128 * (mt + 1)],
                                    sinter[:, 512 * j:512 * (j + 1)],
                                    start=(j == 0), stop=(j == NT - 1))
                            sPE.inc(m)
                        dmm_v = sPE.val
                        swd_use[0] = sPE.val

                        @blk.vector
                        def _(v, mtg=mtg, bd=bd, dmm_v=dmm_v, sbi=sbi,
                              h2t_v=h2t_v):
                            v.wait_ge(sPE.h, dmm_v)
                            if sbi == 0:
                                v.wait_ge(qS.h, h2t_v)
                            sDVE.inc(v.tensor_tensor(
                                sacc[:, SEG * mtg:SEG * (mtg + 1)],
                                pb[bd][:, :],
                                sacc[:, SEG * mtg:SEG * (mtg + 1)], ALU.add))
                        bank_ready[bd] = (sDVE, sDVE.val)
                dmm_last_sb = sPE.val
            fin_v = sDVE.val

            for th in range(2):
                for tt in range(2):
                    ttg = 2 * th + tt
                    for mt in range(NT):
                        @blk.tensor
                        def _(t, p=p, ttg=ttg, mt=mt, fin_v=fin_v):
                            if ttg == 0 and mt == 0:
                                t.wait_ge(sDVE.h, fin_v)
                            wait_bank(t, 6)
                            m = t.transpose(
                                pb[6][:, 0:128],
                                sacc[:, SEG * mt + 128 * ttg:SEG * mt + 128 * (ttg + 1)],
                                sident[:])
                            sPE.inc(m)
                        tp_v = sPE.val

                        @blk.vector
                        def _(v, p=p, th=th, tt=tt, mt=mt, tp_v=tp_v):
                            v.wait_ge(sPE.h, tp_v)
                            if th > 0 and tt == 0 and mt == 0:
                                v.wait_ge(sST.h, sST.val)
                            colbase = D * tt + 256 * (mt // 2) + (mt % 2)
                            sDVE.inc(v.tensor_copy(
                                sout[:, colbase:colbase + 255:2], pb[6][:, 0:128]))
                        bank_ready[6] = (sDVE, sDVE.val)
                st_v = sDVE.val

                @blk.gpsimd
                def _(g, p=p, th=th, st_v=st_v):
                    g.wait_ge(sDVE.h, st_v)
                    r0 = 512 * p + 256 * th
                    g.dma_start(out=out_h[r0:r0 + 128, :], in_=sout[:, 0:D]
                                ).then_inc(sST.h, 16)
                    g.dma_start(out=out_h[r0 + 128:r0 + 256, :], in_=sout[:, D:2 * D]
                                ).then_inc(sST.h, 16)
                    sST.val += 32
            st_prev_pass = sST.val
            st_prev_dve = sDVE.val
            gu_mm_prev_pass = sPE.val

        @blk.gpsimd
        def _(g):
            g.wait_ge(sST.h, sST.val)

    return nc


_NC_CACHE = None


def kernel(L0_attn_x, L0_attn_W_in, L0_attn_conv_w, L0_attn_conv_b, L0_attn_W_x,
           L0_attn_W_dt, L0_attn_b_dt, L0_attn_A_log, L0_attn_D, L0_attn_W_out,
           L0_ffn_gate_W_packed, L0_ffn_gate_scales, L0_ffn_up_W_packed,
           L0_ffn_up_scales, L0_ffn_down_W_packed, L0_ffn_down_scales):
    global _NC_CACHE
    x = np.asarray(L0_attn_x)
    d_order = _rho_order()

    WinT = np.ascontiguousarray(np.asarray(L0_attn_W_in).T)[d_order]
    WxT = np.ascontiguousarray(np.asarray(L0_attn_W_x).T)
    WdtT = np.ascontiguousarray(np.asarray(L0_attn_W_dt).T)
    WoutT = np.ascontiguousarray(np.asarray(L0_attn_W_out).T)[:, d_order]
    convw = np.asarray(L0_attn_conv_w)
    convb = np.asarray(L0_attn_conv_b)
    b_dt = np.asarray(L0_attn_b_dt)
    A_log = np.asarray(L0_attn_A_log)
    Dv = np.asarray(L0_attn_D)

    def t16(a_i32):
        v = np.ascontiguousarray(a_i32).view(np.int16)[:, 0::2]
        return np.ascontiguousarray(v.T)

    gpkT = t16(np.asarray(L0_ffn_gate_W_packed))
    upkT = t16(np.asarray(L0_ffn_up_W_packed))
    dpkT = t16(np.asarray(L0_ffn_down_W_packed))
    gscT = np.ascontiguousarray(np.asarray(L0_ffn_gate_scales).T)
    uscT = np.ascontiguousarray(np.asarray(L0_ffn_up_scales).T)
    dscT = np.ascontiguousarray(np.asarray(L0_ffn_down_scales).T)
    dpkT = dpkT[:, d_order]
    dscT = dscT[:, d_order]

    ident = np.eye(128, dtype=np.float32)

    def dmaj(vec_half):
        return np.ascontiguousarray(vec_half.reshape(NT, 128).T)

    in_maps = []
    for c in range(8):
        beta, rho = c // 2, c % 2
        dsl = slice(rho * EH, (rho + 1) * EH)
        xTc = np.ascontiguousarray(x[beta].T[d_order])
        cols = np.r_[rho * EH:(rho + 1) * EH, E + rho * EH:E + (rho + 1) * EH]
        winT_c = np.ascontiguousarray(WinT[:, cols])
        wx = WxT[dsl]
        wxT_c = np.ascontiguousarray(np.concatenate(
            [wx[:, :144], np.zeros((EH, 16), np.float32), wx[:, 144:]], axis=1))
        nsl = slice(2048 * c, 2048 * (c + 1))
        in_maps.append(dict(
            xT=xTc, winT=winT_c, wxT=wxT_c,
            wdtT=np.ascontiguousarray(WdtT[:, dsl]),
            woutT=np.ascontiguousarray(WoutT[dsl]),
            convw=np.ascontiguousarray(
                convw[dsl].reshape(NT, 128, DC).transpose(1, 0, 2).reshape(
                    128, NT * DC)),
            convb=dmaj(convb[dsl]), bdt=dmaj(b_dt[dsl]),
            alog=np.ascontiguousarray(
                A_log[dsl].reshape(NT, 128, S).transpose(1, 0, 2).reshape(
                    128, NT * S)),
            dvec=dmaj(Dv[dsl]),
            gpk=np.ascontiguousarray(gpkT[:, nsl][:, d_order]),
            upk=np.ascontiguousarray(upkT[:, nsl][:, d_order]),
            dpk=np.ascontiguousarray(dpkT[1024 * c:1024 * (c + 1), :]),
            gsc=np.ascontiguousarray(
                np.repeat(gscT[:, nsl][:, d_order], 16, axis=0)),
            usc=np.ascontiguousarray(
                np.repeat(uscT[:, nsl][:, d_order], 16, axis=0)),
            dsc=np.ascontiguousarray(
                np.repeat(dscT[64 * c:64 * (c + 1), :], 16, axis=0)),
            ident=ident,
        ))

    if _NC_CACHE is None:
        _NC_CACHE = build_nc()
    res = run_bass_kernel_spmd(_NC_CACHE, in_maps, core_ids=list(range(8)))

    out = np.empty((B, T, D), dtype=np.float32)
    for c in range(8):
        beta, rho = c // 2, c % 2
        oc = res.results[c]["out"]
        for p in range(NPASS):
            s = 2 * p + rho
            out[beta, SEG * s:SEG * (s + 1), :] = oc[SEG * p:SEG * (p + 1), :]
    return out


# version 08
# speedup vs baseline: 1.0746x; 1.0331x over previous
import numpy as np
from contextlib import ExitStack

import concourse.bass as bass
import concourse.mybir as mybir
from concourse.bass_utils import run_bass_kernel_spmd

F32 = mybir.dt.float32
BF16 = mybir.dt.bfloat16
I16 = mybir.dt.int16
AF = mybir.ActivationFunctionType
ALU = mybir.AluOpType
ET = mybir.EngineType

B, T, D = 4, 4096, 2048
E, EH, S, R, DC = 4096, 2048, 16, 128, 4
NM = 16384
SEG = 512
NSEG = T // SEG
NT = 16
NPASS = 4
NSB = 8
EPS = 1e-5
GA, GB = 0.7978845608028654, 0.044715
RG8 = [[0, 1, 2, 3, 4, 5, 6, 7]]


def _rho_order():
    q = np.arange(D)
    return (2 * (128 * (q >> 8) + (q & 127)) + ((q >> 7) & 1)).astype(np.int64)


class Sem:
    def __init__(self, nc, es, name):
        self.h = es.enter_context(nc.semaphore(name))
        self.val = 0

    def inc(self, instr, n=1):
        instr.then_inc(self.h, n)
        self.val += n
        return self.val

    def incd(self, eng, n=1):
        eng.drain().then_inc(self.h, n)
        self.val += n
        return self.val


def _load_act_set(nc, sc, name):
    from concourse.hw_specs import get_activation_tables
    for idx, nm in enumerate(get_activation_tables(nc.m.arch).keys()):
        if nm == name:
            return sc.add_instruction(
                mybir.InstLoadActFuncSet(
                    name=nc.get_next_instruction_name(),
                    act_func_set_id=idx, ins=[], outs=[]))
    raise ValueError(name)


def build_nc():
    nc = bass.Bass("TRN2", target_bir_lowering=False)

    xT = nc.dram_tensor("xT", [D, T], F32, kind="ExternalInput").ap()
    winT = nc.dram_tensor("winT", [D, 2 * EH], F32, kind="ExternalInput").ap()
    wxT = nc.dram_tensor("wxT", [EH, 176], F32, kind="ExternalInput").ap()
    wdtT = nc.dram_tensor("wdtT", [R, EH], F32, kind="ExternalInput").ap()
    woutT = nc.dram_tensor("woutT", [EH, D], F32, kind="ExternalInput").ap()
    convw = nc.dram_tensor("convw", [128, NT * DC], F32, kind="ExternalInput").ap()
    convb = nc.dram_tensor("convb", [128, NT], F32, kind="ExternalInput").ap()
    bdt = nc.dram_tensor("bdt", [128, NT], F32, kind="ExternalInput").ap()
    alog = nc.dram_tensor("alog", [128, NT * S], F32, kind="ExternalInput").ap()
    dvec = nc.dram_tensor("dvec", [128, NT], F32, kind="ExternalInput").ap()
    gpk = nc.dram_tensor("gpk", [1024, 2048], I16, kind="ExternalInput").ap()
    upk = nc.dram_tensor("upk", [1024, 2048], I16, kind="ExternalInput").ap()
    dpk = nc.dram_tensor("dpk", [1024, 2048], I16, kind="ExternalInput").ap()
    gsc = nc.dram_tensor("gsc", [1024, 2048], F32, kind="ExternalInput").ap()
    usc = nc.dram_tensor("usc", [1024, 2048], F32, kind="ExternalInput").ap()
    dsc = nc.dram_tensor("dsc", [1024, 2048], F32, kind="ExternalInput").ap()
    ident = nc.dram_tensor("ident", [128, 128], F32, kind="ExternalInput").ap()
    out_h = nc.dram_tensor("out", [2048, D], F32, kind="ExternalOutput").ap()

    RGP = [[0, 1], [2, 3], [4, 5], [6, 7]]
    ag_xd_in = [nc.dram_tensor(f"ag_xd_in{s}", [176, SEG], F32, kind="Internal").ap()
                for s in range(NSEG)]
    ag_xd_out = [nc.dram_tensor(f"ag_xd_out{s}", [176, SEG], F32,
                                kind="Internal").ap()
                 for s in range(NSEG)]
    ag_wo_in = [nc.dram_tensor(f"ag_wo_in{s}", [D, SEG], BF16, kind="Internal").ap()
                for s in range(NSEG)]
    ag_wo_out = [nc.dram_tensor(f"ag_wo_out{s}", [D, SEG], BF16,
                                kind="Internal").ap()
                 for s in range(NSEG)]
    ag_g_in = nc.dram_tensor("ag_g_in", [2048, 2048], BF16, kind="Internal").ap()
    ag_u_in = nc.dram_tensor("ag_u_in", [2048, 2048], BF16, kind="Internal").ap()
    ag_d_in = nc.dram_tensor("ag_d_in", [2048, 2048], BF16, kind="Internal").ap()
    WgD = nc.dram_tensor("WgD", [NM, 2048], BF16, kind="Internal").ap()
    WuD = nc.dram_tensor("WuD", [NM, 2048], BF16, kind="Internal").ap()
    WdD = nc.dram_tensor("WdD", [NM, 2048], BF16, kind="Internal").ap()
    resid_d = nc.dram_tensor("resid_d", [D, T], F32, kind="Internal").ap()
    h2_d = nc.dram_tensor("h2_d", [D, T], BF16, kind="Internal").ap()

    es = ExitStack()
    sb = lambda name, shape, dt: es.enter_context(nc.sbuf_tensor(name, shape, dt))
    with es:
        sxn = sb("sxn", [128, NT * SEG], BF16)
        su = sb("su", [128, NT * SEG], BF16)
        sdt = sb("sdt", [128, NT * SEG], BF16)
        sxs = sb("sxs", [128, NT * (SEG + 3)], BF16)
        arA = sb("arA", [128, NT * 2048], BF16)
        arB = sb("arB", [128, NT * SEG], F32)
        sconvw = sb("sconvw", [128, NT * DC], F32)
        sconvb = sb("sconvb", [128, NT], F32)
        sconvbn = sb("sconvbn", [128, NT], F32)
        sbdt = sb("sbdt", [128, NT], F32)
        sdvec = sb("sdvec", [128, NT], F32)
        sA = sb("sA", [128, NT * S], F32)
        swx = sb("swx", [128, NT * 176], BF16)
        swdt = sb("swdt", [128, EH], BF16)
        sident = sb("sident", [128, 128], F32)
        sones = sb("sones", [128, 1], BF16)
        sonef = sb("sonef", [1, 128], F32)
        scarry = sb("scarry", [128, NT * S], F32)
        sxpa = sb("sxpa", [128, SEG], F32)
        sxpb = sb("sxpb", [48, SEG], F32)
        sxda = sxpa
        sxdb = sxpb
        sdtr = sb("sdtr", [128, SEG], BF16)
        st_mu = sb("st_mu", [1, SEG], F32)
        st_e2 = sb("st_e2", [1, SEG], F32)
        st_t1 = sb("st_t1", [1, SEG], F32)
        st_a = sb("st_a", [1, SEG], F32)
        st_b = sb("st_b", [1, SEG], F32)
        sab_a = sb("sab_a", [128, SEG], F32)
        sab_b = sb("sab_b", [128, SEG], F32)
        sct = sb("sct", [128, 2 * SEG], F32)
        stmp = sb("stmp", [128, 2 * SEG], F32)
        szs = sb("szs", [128, 2 * SEG], BF16)
        stmp2 = sct
        sevac = szs


        swin = arA[:]
        sh = arA[:, 0:S * SEG]
        sdA = [arA[:, S * SEG:2 * S * SEG],
               arA[:, 2 * S * SEG:3 * S * SEG]]
        sgx = [arA[:, 0:SEG], arA[:, SEG:2 * SEG]]
        sux = [arA[:, 2 * SEG:3 * SEG], arA[:, 3 * SEG:4 * SEG]]
        sqx = [arA[:, 4 * SEG:5 * SEG], arA[:, 5 * SEG:6 * SEG]]
        srx = [arA[:, 6 * SEG:7 * SEG], arA[:, 7 * SEG:8 * SEG]]
        sinter = arA[:, 8 * SEG:(8 + NT) * SEG]
        sw_t = arA[:, 3 * S * SEG:3 * S * SEG + SEG]
        sdbu = arA[:, 3 * S * SEG + SEG:3 * S * SEG + 2 * SEG]

        arBb = arB[:].bitcast(BF16)
        sBbc = arBb[:, 0:S * SEG]
        sCbc = arBb[:, S * SEG:2 * S * SEG]
        sxf = arB[:]
        sacc = arB[:]
        arBi = arB[:].bitcast(I16)
        sdk = arBi[:, 0:1024]
        sdn = [arBi[:, 1024:2048], arBi[:, 2048:3072]]
        sdm = arBi[:, 3072:4096]
        sdq = arBi[:, 4096:5120]
        sdz = arBi[:, 5120:6144]
        sdsc = arB[:, 3072:4096]
        sdv = [arBb[:, 8192:9216], arBb[:, 9216:10240]]
        smine = sdt[:]
        sresb = sdt[:]
        swd = sdt[:]
        swg = [sxs[:, 0:NT * 128], sxs[:, NT * 128:2 * NT * 128]]
        swu = [sxs[:, 2 * NT * 128:3 * NT * 128], sxs[:, 3 * NT * 128:4 * NT * 128]]
        sh2t = sxn[:]
        sout = su[:].bitcast(F32)

        pb = [es.enter_context(nc.psum_tensor(f"pb{i}", [128, 512], F32))
              for i in range(8)]

        qS = Sem(nc, es, "qS")
        qG = Sem(nc, es, "qG")
        sPE = Sem(nc, es, "sPE")
        sACT = Sem(nc, es, "sACT")
        sDVE = Sem(nc, es, "sDVE")
        sST = Sem(nc, es, "sST")
        sCCg = Sem(nc, es, "sCCg")
        sCCu = Sem(nc, es, "sCCu")
        sCCd = Sem(nc, es, "sCCd")
        sCCx = [Sem(nc, es, f"sCCx{i}") for i in range(NSEG)]
        sCCw = [Sem(nc, es, f"sCCw{i}") for i in range(NSEG)]

        seps = sb("seps", [128, 1], F32)
        sone_b = sb("sone_b", [128, 1], F32)

        blk = es.enter_context(nc.Block())

        bank_ready = [(sACT, 0) for _ in range(8)]

        def wait_bank(t, b):
            sem, val = bank_ready[b]
            if val > 0:
                t.wait_ge(sem.h, val)

        def seg_sl(s):
            return slice(SEG * s, SEG * (s + 1))

        def dm3(dram_ap, w):
            return dram_ap.rearrange("(i p) t -> p i t", p=128)

        def sb3(sbuf_ap, w):
            return sbuf_ap.rearrange("p (i t) -> p i t", t=w)

        @blk.sync
        def _(sync):
            for src, dst in [(convw, sconvw), (convb, sconvb), (bdt, sbdt),
                             (dvec, sdvec), (alog, sA), (ident, sident)]:
                sync.dma_start(out=dst[:], in_=src[:]).then_inc(qS.h, 16)
                qS.val += 16

        @blk.gpsimd
        def _(g):
            g.dma_start(out=sb3(swx[:], 176), in_=dm3(wxT[:], 176)
                        ).then_inc(qG.h, 16)
            g.dma_start(out=swdt[:], in_=wdtT[:]).then_inc(qG.h, 16)
            qG.val += 32

        @blk.vector
        def _(v):
            v.memset(sones[:], 1.0)
            v.memset(sonef[:], 1.0)
            v.memset(seps[:], EPS)
            v.memset(sone_b[:], 1.0)
            v.wait_ge(qS.h, qS.val)
            v.tensor_scalar(sconvbn[:], sconvb[:], -1.0, None, ALU.mult)
            v.memset(scarry[:], 0.0)
            hdr = sxs[:].rearrange("p (i w) -> p i w", w=SEG + 3)
            sDVE.inc(v.memset(hdr[:, :, 0:3], 0.0))
        init_dve = sDVE.val

        @blk.scalar
        def _(sc):
            sc.wait_ge(qS.h, qS.val)
            sc.wait_ge(sDVE.h, init_dve)
            _load_act_set(nc, sc, "natural_log_exp_and_others")
            sc.activation(sA[:], sA[:], AF.Exp)
            sACT.inc(sc.activation(sA[:], sA[:], AF.Copy, scale=-1.0))

        dec_dve = [0]
        for mat, (pk_h, sc_h, agin) in (("g", (gpk, gsc, ag_g_in)),
                                        ("u", (upk, usc, ag_u_in)),
                                        ("d", (dpk, dsc, ag_d_in))):
            for t8 in range(8):
                for ch in range(2):
                    csl = slice(1024 * ch, 1024 * (ch + 1))

                    @blk.gpsimd
                    def _(g, t8=t8, csl=csl):
                        if dec_dve[0] > 0:
                            g.wait_ge(sDVE.h, dec_dve[0])
                        g.dma_start(out=sdk[:], in_=pk_h[128 * t8:128 * (t8 + 1), csl]
                                    ).then_inc(qG.h, 16)
                        g.dma_start(out=sdsc[:],
                                    in_=sc_h[128 * t8:128 * (t8 + 1), csl]
                                    ).then_inc(qG.h, 16)
                        qG.val += 32

                    @blk.vector
                    def _(v, t8=t8):
                        v.wait_ge(qG.h, qG.val)
                        v.tensor_scalar(sdn[0], sdk[:], 15, None, ALU.bitwise_and)
                        v.tensor_scalar(sdn[1], sdk[:], 4, None,
                                        ALU.logical_shift_right)
                        for nb in range(2):
                            c = sdn[nb]
                            v.tensor_scalar(sdm[:], c, 7, None, ALU.bitwise_and)
                            v.tensor_scalar(sdz[:], sdm[:], 2, None, ALU.is_ge)
                            v.tensor_tensor(sdq[:], sdm[:], sdz[:], ALU.add)
                            v.tensor_scalar(sdq[:], sdq[:], 0x40, 0x3EC0,
                                            ALU.mult, ALU.add)
                            v.tensor_scalar(sdz[:], sdm[:], 1, None, ALU.is_ge)
                            v.tensor_tensor(sdq[:], sdq[:], sdz[:], ALU.mult)
                            v.tensor_scalar(sdz[:], c, 8, 12, ALU.bitwise_and,
                                            ALU.logical_shift_left)
                            v.tensor_tensor(sdq[:], sdq[:], sdz[:], ALU.add)
                            last = v.tensor_tensor(sdv[nb], sdq[:].bitcast(BF16),
                                                   sdsc[:], ALU.mult)
                            if nb == 1:
                                sDVE.inc(last)
                        dec_dve[0] = sDVE.val

                    @blk.gpsimd
                    def _(g, t8=t8, ch=ch, mat=mat):
                        g.wait_ge(sDVE.h, dec_dve[0])
                        for nb in range(2):
                            if mat != "d":
                                kt = 2 * t8 + nb
                                dst = agin.rearrange("(a p) (i n) -> p a i n",
                                                     p=128, n=128)
                                dst = dst[:, 8 * ch:8 * (ch + 1), kt, :]
                                src = sdv[nb].rearrange("p (a n) -> p a n", n=128)
                            else:
                                r0 = 256 * t8 + 128 * nb
                                dst = agin[r0:r0 + 128, 1024 * ch:1024 * (ch + 1)]
                                src = sdv[nb]
                            g.dma_start(out=dst, in_=src).then_inc(qG.h, 16)
                            qG.val += 16

        @blk.gpsimd
        def _(g):
            g.wait_ge(qG.h, qG.val)
            sCCg.inc(g.collective_compute(
                "AllGather", ALU.bypass, replica_groups=RG8,
                ins=[ag_g_in[:].opt()], outs=[WgD[:].opt()]))
            sCCu.inc(g.collective_compute(
                "AllGather", ALU.bypass, replica_groups=RG8,
                ins=[ag_u_in[:].opt()], outs=[WuD[:].opt()]))
            sCCd.inc(g.collective_compute(
                "AllGather", ALU.bypass, replica_groups=RG8,
                ins=[ag_d_in[:].opt()], outs=[WdD[:].opt()]))

        dec_done_dve = sDVE.val
        dec_done_qg = qG.val

        xn_ready = [0] * NSEG
        u_ready = [0] * NSEG
        hdr_ready = [0] * NSEG
        bc_done = [0] * NSEG
        dt_ready = [0] * NSEG
        scan_done = [0] * NSEG
        p_done = [0] * NSEG
        res_done = [0] * NSEG
        h2_done = [0] * NSEG
        wout_mm_done = [0] * NSEG
        zmm_done = [0] * NSEG
        xdmm_done = [0] * NSEG
        agin_dma = [0] * NSEG
        xdsum_done = [0] * NSEG
        dtmm_done = [0] * NSEG
        dAlast = [0] * NSEG

        def emit_ln(tag, src_bf, sq_buf, dst_bf, s):
            @blk.scalar
            def _(sc):
                sACT.inc(sc.activation(sq_buf, src_bf, AF.Square))
            sqv = sACT.val

            @blk.tensor
            def _(t):
                t.wait_ge(sACT.h, sqv)
                wait_bank(t, 6)
                wait_bank(t, 7)
                for i in range(NT):
                    t.matmul(pb[6][0:1, :], sones[:], src_bf[:, 512 * i:512 * (i + 1)],
                             start=(i == 0), stop=(i == NT - 1))
                for i in range(NT):
                    m = t.matmul(pb[7][0:1, :], sones[:],
                                 sq_buf[:, 512 * i:512 * (i + 1)],
                                 start=(i == 0), stop=(i == NT - 1))
                sPE.inc(m)
            stv = sPE.val

            @blk.vector
            def _(v):
                v.wait_ge(sPE.h, stv)
                v.tensor_scalar(st_mu[:], pb[6][0:1, :], 1.0 / D, None, ALU.mult)
                v.tensor_scalar(st_e2[:], pb[7][0:1, :], 1.0 / D, None, ALU.mult)
                v.tensor_tensor(st_t1[:], st_mu[:], st_mu[:], ALU.mult)
                sDVE.inc(v.tensor_tensor(st_t1[:], st_e2[:], st_t1[:], ALU.subtract))
            varv = sDVE.val

            @blk.scalar
            def _(sc):
                sc.wait_ge(sDVE.h, varv)
                sc.activation(st_t1[:], st_t1[:], AF.Ln, bias=seps[0:1, 0:1])
                sACT.inc(sc.activation(st_a[:], st_t1[:], AF.Exp, scale=-0.5))
            rsv = sACT.val

            @blk.vector
            def _(v):
                v.wait_ge(sACT.h, rsv)
                v.tensor_tensor(st_b[:], st_mu[:], st_a[:], ALU.mult)
                sDVE.inc(v.tensor_scalar(st_b[:], st_b[:], -1.0, None, ALU.mult))
            abv = sDVE.val

            @blk.tensor
            def _(t):
                t.wait_ge(sDVE.h, abv)
                t.matmul(pb[6][:, :], sonef[:], st_a[:], start=True, stop=True)
                m = t.matmul(pb[7][:, :], sonef[:], st_b[:], start=True, stop=True)
                sPE.inc(m)
            bcv = sPE.val

            @blk.scalar
            def _(sc):
                sc.wait_ge(sPE.h, bcv)
                sc.copy(sab_a[:], pb[6][:, :])
                sACT.inc(sc.copy(sab_b[:], pb[7][:, :]))
            bank_ready[6] = (sACT, sACT.val)
            bank_ready[7] = (sACT, sACT.val)
            abv2 = sACT.val

            @blk.vector
            def _(v):
                v.wait_ge(sACT.h, abv2)
                last = None
                for i in range(NT):
                    dsl = slice(SEG * i, SEG * (i + 1))
                    v.tensor_tensor(dst_bf[:, dsl], src_bf[:, dsl], sab_a[:],
                                    ALU.mult)
                    last = v.tensor_tensor(dst_bf[:, dsl], dst_bf[:, dsl],
                                           sab_b[:], ALU.add)
                sDVE.inc(last)
            return sDVE.val

        for s in range(NSEG):
            @blk.sync
            def _(sync, s=s):
                if s > 0:
                    sync.wait_ge(qG.h, h2_done[s - 1])
                else:
                    sync.wait_ge(sDVE.h, dec_done_dve)
                    sync.wait_ge(qG.h, dec_done_qg)
                src = dm3(xT[:, seg_sl(s)], SEG)
                sync.dma_start(out=sb3(sxf[:, 0:NT * SEG], SEG),
                               in_=src).then_inc(qS.h, 16)
                qS.val += 16
            xload_v = qS.val

            @blk.gpsimd
            def _(g, s=s):
                if s > 0:
                    g.wait_ge(sPE.h, wout_mm_done[s - 1])
                for i in range(NT):
                    g.dma_start(out=swin[:, 2048 * i:2048 * (i + 1)],
                                in_=winT[128 * i:128 * (i + 1), 0:2048]
                                ).then_inc(qG.h, 16)
                    qG.val += 16
            wld_f = qG.val

            @blk.vector
            def _(v, s=s):
                v.wait_ge(qS.h, xload_v)
                if s > 0:
                    v.wait_ge(sPE.h, zmm_done[s - 1])
                    v.wait_ge(qG.h, h2_done[s - 1])
                sDVE.inc(v.tensor_copy(sxn[:], sxf[:, 0:NT * SEG]))
            @blk.scalar
            def _(sc, s=s):
                sc.wait_ge(sDVE.h, sDVE.val)
                if s > 0:
                    sc.wait_ge(sPE.h, wout_mm_done[s - 1])
            xn_ready[s] = emit_ln(f"ln1_{s}", sxn[:], su[:], sxn[:], s)

            for j in range(NT):
                bj = j % 2
                @blk.tensor
                def _(t, s=s, j=j, bj=bj):
                    if j == 0:
                        t.wait_ge(sDVE.h, xn_ready[s])
                        t.wait_ge(qG.h, wld_f)
                        if s > 0:
                            t.wait_ge(sDVE.h, hdr_ready[s - 1])
                    wait_bank(t, bj)
                    for i in range(NT):
                        m = t.matmul(pb[bj][:, :],
                                     swin[:, 2048 * i + 128 * j:2048 * i + 128 * (j + 1)],
                                     sxn[:, 512 * i:512 * (i + 1)],
                                     start=(i == 0), stop=(i == NT - 1))
                    sPE.inc(m)
                mmv = sPE.val

                @blk.scalar
                def _(sc, s=s, j=j, bj=bj):
                    sc.wait_ge(sPE.h, mmv)
                    sACT.inc(sc.copy(sxs[:, (SEG + 3) * j + 3:(SEG + 3) * (j + 1)],
                                     pb[bj][:, :]))
                bank_ready[bj] = (sACT, sACT.val)
                evv = sACT.val

                @blk.vector
                def _(v, s=s, j=j):
                    v.wait_ge(sACT.h, evv)
                    base = (SEG + 3) * j
                    cp = j % 2
                    v.tensor_scalar(sct[:, SEG * cp:SEG * (cp + 1)],
                                    sxs[:, base:base + SEG],
                                    sconvw[:, DC * j:DC * j + 1], None, ALU.mult)
                    for k in range(1, DC):
                        v.scalar_tensor_tensor(
                            sct[:, SEG * cp:SEG * (cp + 1)],
                            sxs[:, base + k:base + k + SEG],
                            sconvw[:, DC * j + k:DC * j + k + 1],
                            sct[:, SEG * cp:SEG * (cp + 1)], ALU.mult, ALU.add)
                    sDVE.inc(v.tensor_copy(sct[0:1, SEG * cp:SEG * cp + 1],
                                           sct[0:1, SEG * cp:SEG * cp + 1]))
                cvv = sDVE.val

                @blk.scalar
                def _(sc, s=s, j=j):
                    sc.wait_ge(sDVE.h, cvv)
                    cp = j % 2
                    csl = slice(SEG * cp, SEG * (cp + 1))
                    sc.activation(stmp[:, csl], sct[:, csl], AF.Exp, scale=-1.0,
                                  bias=sconvbn[:, j:j + 1])
                    sc.activation(stmp[:, csl], stmp[:, csl],
                                  AF.Ln, bias=sone_b[:, 0:1])
                    sACT.inc(sc.activation(stmp[:, csl], stmp[:, csl],
                                           AF.Exp, scale=-1.0))
                sgv = sACT.val

                @blk.vector
                def _(v, s=s, j=j):
                    v.wait_ge(sACT.h, sgv)
                    cp = j % 2
                    csl = slice(SEG * cp, SEG * (cp + 1))
                    sDVE.inc(v.scalar_tensor_tensor(
                        su[:, SEG * j:SEG * (j + 1)], sct[:, csl],
                        sconvb[:, j:j + 1],
                        stmp[:, csl],
                        ALU.add, ALU.mult))
            u_ready[s] = sDVE.val

            @blk.vector
            def _(v, s=s):
                x3s = sxs[:].rearrange("p (i w) -> p i w", w=SEG + 3)
                sDVE.inc(v.tensor_copy(x3s[:, :, 0:3], x3s[:, :, SEG:SEG + 3]))
            hdr_ready[s] = sDVE.val

            @blk.tensor
            def _(t, s=s):
                t.wait_ge(sDVE.h, u_ready[s])
                wait_bank(t, 2)
                wait_bank(t, 3)
                for i in range(NT):
                    t.matmul(pb[2][:, :], swx[:, 176 * i:176 * i + 128],
                             su[:, 512 * i:512 * (i + 1)],
                             start=(i == 0), stop=(i == NT - 1))
                for i in range(NT):
                    m = t.matmul(pb[3][0:48, :], swx[:, 176 * i + 128:176 * (i + 1)],
                                 su[:, 512 * i:512 * (i + 1)],
                                 start=(i == 0), stop=(i == NT - 1))
                sPE.inc(m)
            xdmm_done[s] = sPE.val

            @blk.scalar
            def _(sc, s=s):
                sc.wait_ge(sPE.h, xdmm_done[s])
                if s > 0:
                    sc.wait_ge(qG.h, bc_done[s - 1])
                sc.copy(sxpa[:], pb[2][:, :])
                sACT.inc(sc.copy(sxpb[:], pb[3][0:48, :]))
            bank_ready[2] = (sACT, sACT.val)
            bank_ready[3] = (sACT, sACT.val)
            xdev = sACT.val

            @blk.gpsimd
            def _(g, s=s):
                g.wait_ge(sACT.h, xdev)
                g.dma_start(out=ag_xd_in[s][0:128, :], in_=sxpa[:]).then_inc(qG.h, 16)
                g.dma_start(out=ag_xd_in[s][128:176, :], in_=sxpb[:]).then_inc(qG.h, 16)
                qG.val += 32
                agin_dma[s] = qG.val
                g.wait_ge(qG.h, qG.val)
                sCCx[s].inc(g.collective_compute(
                    "AllReduce", ALU.add, replica_groups=RGP,
                    ins=[ag_xd_in[s][:].opt()], outs=[ag_xd_out[s][:].opt()]))

            @blk.gpsimd
            def _(g, s=s):
                g.wait_ge(sCCx[s].h, 1)
                if s > 0:
                    g.wait_ge(sPE.h, dtmm_done[s - 1])
                    g.wait_ge(sDVE.h, res_done[s - 1])
                    g.wait_ge(qG.h, h2_done[s - 1])
                g.dma_start(out=sdtr[:], in_=ag_xd_out[s][0:128, :]
                            ).then_inc(qG.h, 16)
                srcB = ag_xd_out[s][128:144, :].unsqueeze(0).broadcast_to(
                    [128, 16, SEG])
                g.dma_start(out=sBbc.rearrange("p (g t) -> p g t", t=SEG),
                            in_=srcB).then_inc(qG.h, 16)
                srcC = ag_xd_out[s][160:176, :].unsqueeze(0).broadcast_to(
                    [128, 16, SEG])
                g.dma_start(out=sCbc.rearrange("p (g t) -> p g t", t=SEG),
                            in_=srcC).then_inc(qG.h, 16)
                qG.val += 48
            bc_done[s] = qG.val
            xdsum_done[s] = 0

            for j in range(NT):
                bj = 4 + (j % 2)
                jp = j % 2
                @blk.tensor
                def _(t, s=s, j=j, bj=bj):
                    if j == 0:
                        t.wait_ge(qG.h, bc_done[s])
                    wait_bank(t, bj)
                    m = t.matmul(pb[bj][:, :], swdt[:, 128 * j:128 * (j + 1)],
                                 sdtr[:], start=True, stop=True)
                    sPE.inc(m)
                mmv = sPE.val

                @blk.scalar
                def _(sc, s=s, j=j, bj=bj, jp=jp):
                    sc.wait_ge(sPE.h, mmv)
                    if j >= 2:
                        sc.wait_ge(sDVE.h, dt_tt[jp])
                    tsl = slice(SEG * jp, SEG * (jp + 1))
                    sc.activation(stmp[:, tsl], pb[bj][:, :], AF.Abs,
                                  bias=sbdt[:, j:j + 1])
                    sc.activation(stmp[:, tsl], stmp[:, tsl], AF.Exp, scale=-1.0)
                    sc.activation(stmp[:, tsl], stmp[:, tsl], AF.Ln, bias=sone_b[:, 0:1])
                    sACT.inc(sc.activation(stmp2[:, tsl], pb[bj][:, :], AF.Relu,
                                           bias=sbdt[:, j:j + 1]))
                spv = sACT.val
                bank_ready[bj] = (sACT, sACT.val)

                @blk.vector
                def _(v, s=s, j=j, jp=jp):
                    v.wait_ge(sACT.h, spv)
                    tsl = slice(SEG * jp, SEG * (jp + 1))
                    sDVE.inc(v.tensor_tensor(sdt[:, SEG * j:SEG * (j + 1)],
                                             stmp[:, tsl], stmp2[:, tsl], ALU.add))
                if j == 0:
                    dt_tt = [0, 0]
                dt_tt[jp] = sDVE.val
            dt_ready[s] = sDVE.val
            dtmm_done[s] = sPE.val

            dA_wr = [0, 0]
            dA_rd = [0, 0]
            for j in range(NT):
                pp = j % 2
                @blk.scalar
                def _(sc, s=s, j=j, pp=pp):
                    if j == 0:
                        sc.wait_ge(sDVE.h, dt_ready[s])
                    if dA_rd[pp] > 0:
                        sc.wait_ge(sDVE.h, dA_rd[pp])
                    last = None
                    for sig in range(S):
                        last = sc.activation(
                            sdA[pp][:, SEG * sig:SEG * (sig + 1)],
                            sdt[:, SEG * j:SEG * (j + 1)], AF.Exp,
                            scale=sA[:, S * j + sig:S * j + sig + 1])
                    sACT.inc(last)
                dA_wr[pp] = sACT.val

                @blk.vector
                def _(v, s=s, j=j, pp=pp):
                    if j == 0:
                        v.wait_ge(qG.h, bc_done[s])
                    v.wait_ge(sACT.h, dA_wr[pp])
                    v.tensor_tensor(sw_t[:], sdt[:, SEG * j:SEG * (j + 1)],
                                    su[:, SEG * j:SEG * (j + 1)], ALU.mult)
                    for sig in range(S):
                        ssl = slice(SEG * sig, SEG * (sig + 1))
                        v.tensor_tensor(sdbu[:], sw_t[:], sBbc[:, ssl], ALU.mult)
                        init = 0.0 if s == 0 else \
                            scarry[:, S * j + sig:S * j + sig + 1]
                        v.tensor_tensor_scan(sh[:, ssl], sdA[pp][:, ssl], sdbu[:],
                                             init, ALU.mult, ALU.add)
                    hh = sh.rearrange("p (g t) -> p g t", t=SEG)
                    v.tensor_copy(scarry[:, S * j:S * (j + 1)], hh[:, :, SEG - 1])
                    v.tensor_tensor(sdA[pp], sh, sCbc, ALU.mult)
                    half = S * SEG // 2
                    while half >= SEG:
                        v.tensor_tensor(sdA[pp][:, 0:half], sdA[pp][:, 0:half],
                                        sdA[pp][:, half:2 * half], ALU.add)
                        half //= 2
                    last = v.scalar_tensor_tensor(
                        su[:, SEG * j:SEG * (j + 1)], su[:, SEG * j:SEG * (j + 1)],
                        sdvec[:, j:j + 1], sdA[pp][:, 0:SEG], ALU.mult, ALU.add)
                    sDVE.inc(last)
                    dA_rd[pp] = sDVE.val
            scan_done[s] = sDVE.val
            dAlast[s] = sACT.val

            @blk.gpsimd
            def _(g, s=s):
                g.wait_ge(sPE.h, xdmm_done[s])
                g.wait_ge(sDVE.h, scan_done[s])
                for i in range(NT):
                    g.dma_start(out=swin[:, 2048 * i:2048 * (i + 1)],
                                in_=winT[128 * i:128 * (i + 1), 2048:4096]
                                ).then_inc(qG.h, 16)
                    qG.val += 16
            zw_v = qG.val

            for j in range(NT):
                bj = j % 2
                jp = j % 2
                @blk.tensor
                def _(t, s=s, j=j, bj=bj):
                    if j == 0:
                        t.wait_ge(qG.h, zw_v)
                    wait_bank(t, bj)
                    for i in range(NT):
                        m = t.matmul(pb[bj][:, :],
                                     swin[:, 2048 * i + 128 * j:2048 * i + 128 * (j + 1)],
                                     sxn[:, 512 * i:512 * (i + 1)],
                                     start=(i == 0), stop=(i == NT - 1))
                    sPE.inc(m)
                mmv = sPE.val

                @blk.scalar
                def _(sc, s=s, j=j, bj=bj, jp=jp):
                    sc.wait_ge(sPE.h, mmv)
                    if j >= 2:
                        sc.wait_ge(sDVE.h, p_tt[jp])
                    csl = slice(SEG * jp, SEG * (jp + 1))
                    sc.activation(stmp[:, csl], pb[bj][:, :], AF.Exp, scale=-1.0)
                    sc.activation(stmp[:, csl], stmp[:, csl],
                                  AF.Ln, bias=sone_b[:, 0:1])
                    sACT.inc(sc.activation(stmp[:, csl], stmp[:, csl],
                                           AF.Exp, scale=-1.0))
                zev = sACT.val

                @blk.vector
                def _(v, s=s, j=j, jp=jp, bj=bj):
                    v.wait_ge(sACT.h, zev)
                    if j == 0:
                        v.wait_ge(sDVE.h, scan_done[s])
                    csl = slice(SEG * jp, SEG * (jp + 1))
                    v.tensor_tensor(szs[:, csl], pb[bj][:, :],
                                    stmp[:, csl], ALU.mult)
                    sDVE.inc(v.tensor_tensor(su[:, SEG * j:SEG * (j + 1)],
                                             su[:, SEG * j:SEG * (j + 1)],
                                             szs[:, csl],
                                             ALU.mult))
                if j == 0:
                    p_tt = [0, 0]
                p_tt[jp] = sDVE.val
                bank_ready[bj] = (sDVE, sDVE.val)
            p_done[s] = sDVE.val
            zmm_done[s] = sPE.val

            @blk.gpsimd
            def _(g, s=s):
                g.wait_ge(sPE.h, zmm_done[s])
                for i in range(NT):
                    g.dma_start(out=swin[:, 2048 * i:2048 * (i + 1)],
                                in_=woutT[128 * i:128 * (i + 1), :]
                                ).then_inc(qG.h, 16)
                    qG.val += 16
            wow_v = qG.val

            ag_wo_dma = 0
            for mt in range(NT):
                bj = 2 + (mt % 2)
                mp = mt % 2
                @blk.tensor
                def _(t, s=s, mt=mt, bj=bj):
                    if mt == 0:
                        t.wait_ge(qG.h, wow_v)
                        t.wait_ge(sDVE.h, p_done[s])
                    wait_bank(t, bj)
                    for i in range(NT):
                        m = t.matmul(pb[bj][:, :],
                                     swin[:, 2048 * i + 128 * mt:2048 * i + 128 * (mt + 1)],
                                     su[:, 512 * i:512 * (i + 1)],
                                     start=(i == 0), stop=(i == NT - 1))
                    sPE.inc(m)
                mmv = sPE.val

                @blk.scalar
                def _(sc, s=s, mt=mt, bj=bj, mp=mp):
                    sc.wait_ge(sPE.h, mmv)
                    if mt >= 2:
                        sc.wait_ge(qG.h, wo_dma[mp])
                    sACT.inc(sc.copy(sevac[:, SEG * mp:SEG * (mp + 1)],
                                     pb[bj][:, :]))
                bank_ready[bj] = (sACT, sACT.val)
                wev = sACT.val

                @blk.gpsimd
                def _(g, s=s, mt=mt, mp=mp):
                    g.wait_ge(sACT.h, wev)
                    g.dma_start(out=ag_wo_in[s][128 * mt:128 * (mt + 1), :],
                                in_=sevac[:, SEG * mp:SEG * (mp + 1)]
                                ).then_inc(qG.h, 16)
                    qG.val += 16
                if mt == 0:
                    wo_dma = [0, 0]
                wo_dma[mp] = qG.val
            wout_mm_done[s] = sPE.val

            @blk.gpsimd
            def _(g, s=s):
                g.wait_ge(qG.h, qG.val)
                sCCw[s].inc(g.collective_compute(
                    "AllReduce", ALU.add, replica_groups=RGP,
                    ins=[ag_wo_in[s][:].opt()], outs=[ag_wo_out[s][:].opt()]))

            @blk.sync
            def _(sync, s=s):
                sync.wait_ge(sCCw[s].h, 1)
                sync.wait_ge(sACT.h, dAlast[s])
                sync.dma_start(out=sb3(smine[:], SEG),
                               in_=dm3(ag_wo_out[s][:], SEG)).then_inc(qS.h, 16)
                src3 = dm3(xT[:, seg_sl(s)], SEG)
                sync.dma_start(out=sb3(sxf[:, 0:NT * SEG], SEG),
                               in_=src3).then_inc(qS.h, 16)
                qS.val += 32
            agw_read = qS.val

            @blk.vector
            def _(v, s=s):
                v.wait_ge(qS.h, agw_read)
                sDVE.inc(v.tensor_tensor(sxf[:, 0:NT * SEG], sxf[:, 0:NT * SEG],
                                         smine[:], ALU.add))
            res_done[s] = sDVE.val

            @blk.vector
            def _(v, s=s):
                sDVE.inc(v.tensor_copy(sresb[:], sxf[:, 0:NT * SEG]))
            resb_v = sDVE.val

            @blk.scalar
            def _(sc, s=s):
                sc.wait_ge(sDVE.h, resb_v)
            h2v = emit_ln(f"ln2_{s}", sresb[:], sxn[:], sxn[:], s)

            @blk.gpsimd
            def _(g, s=s):
                g.wait_ge(sDVE.h, h2v)
                dst1 = dm3(resid_d[:, seg_sl(s)], SEG)
                g.dma_start(out=dst1, in_=sb3(sxf[:, 0:NT * SEG], SEG)
                            ).then_inc(qG.h, 16)
                dst2 = dm3(h2_d[:, seg_sl(s)], SEG)
                g.dma_start(out=dst2, in_=sb3(sxn[:], SEG)).then_inc(qG.h, 16)
                qG.val += 32
            h2_done[s] = qG.val

        @blk.sync
        def _(sync):
            sync.wait_ge(sCCg.h, 1)
            sync.wait_ge(sCCu.h, 1)
            sync.wait_ge(sCCd.h, 1)

        @blk.scalar
        def _(sc):
            _load_act_set(nc, sc, "exp_and_others")

        gu_mm_prev_pass = 0
        gu_rd = [0, 0]
        swd_use = [0]
        st_prev_pass = 0
        st_prev_dve = 0
        ffn_gate_dve = sDVE.val
        ffn_gate_act = sACT.val
        ffn_gate_pe = sPE.val
        for p in range(NPASS):
            @blk.sync
            def _(sync, p=p):
                sync.wait_ge(qG.h, h2_done[NSEG - 1])
                if p == 0:
                    sync.wait_ge(sDVE.h, ffn_gate_dve)
                    sync.wait_ge(sACT.h, ffn_gate_act)
                    sync.wait_ge(sPE.h, ffn_gate_pe)
                else:
                    sync.wait_ge(sPE.h, gu_mm_prev_pass)
                    sync.wait_ge(sDVE.h, st_prev_dve)
                pid = nc.partition_id([ET.SP])
                for r in sync.Switch(pid, 8):
                    sseg = 2 * p + (int(r) % 2)
                    src = dm3(h2_d[:, SEG * sseg:SEG * (sseg + 1)], SEG)
                    sync.dma_start(out=sb3(sh2t[:], SEG), in_=src
                                   ).then_inc(qS.h, 16)
                    src2 = dm3(resid_d[:, SEG * sseg:SEG * (sseg + 1)], SEG)
                    sync.dma_start(out=sb3(sacc[:, 0:NT * SEG], SEG), in_=src2
                                   ).then_inc(qS.h, 16)
                qS.val += 32
            h2t_v = qS.val

            dmm_last_sb = 0
            wg_use = [0, 0]
            for sbi in range(NSB):
                inter_last = 0
                for j in range(NT):
                    ntile = NT * sbi + j
                    pp2 = ntile % 2

                    @blk.gpsimd
                    def _(g, p=p, ntile=ntile, pp2=pp2):
                        if wg_use[pp2] > 0:
                            g.wait_ge(sPE.h, wg_use[pp2])
                        g.dma_start(out=swg[pp2],
                                    in_=WgD[128 * ntile:128 * (ntile + 1), :]
                                    ).then_inc(qG.h, 16)
                        g.dma_start(out=swu[pp2],
                                    in_=WuD[128 * ntile:128 * (ntile + 1), :]
                                    ).then_inc(qG.h, 16)
                        qG.val += 32
                    wld_v = qG.val

                    bg, bu = 0 + pp2, 2 + pp2
                    @blk.tensor
                    def _(t, p=p, pp2=pp2, bg=bg, bu=bu, wld_v=wld_v):
                        t.wait_ge(qG.h, wld_v)
                        t.wait_ge(qS.h, h2t_v)
                        wait_bank(t, bg)
                        wait_bank(t, bu)
                        for i in range(NT):
                            t.matmul(pb[bg][:, :], swg[pp2][:, 128 * i:128 * (i + 1)],
                                     sh2t[:, 512 * i:512 * (i + 1)],
                                     start=(i == 0), stop=(i == NT - 1))
                        for i in range(NT):
                            m = t.matmul(pb[bu][:, :],
                                         swu[pp2][:, 128 * i:128 * (i + 1)],
                                         sh2t[:, 512 * i:512 * (i + 1)],
                                         start=(i == 0), stop=(i == NT - 1))
                        sPE.inc(m)
                    wg_use[pp2] = sPE.val
                    gumm_v = sPE.val

                    @blk.scalar
                    def _(sc, j=j, bg=bg, bu=bu, pp2=pp2):
                        sc.wait_ge(sPE.h, gumm_v)
                        if gu_rd[pp2] > 0:
                            sc.wait_ge(sDVE.h, gu_rd[pp2])
                        sc.copy(sgx[pp2], pb[bg][:, :])
                        sc.activation(sqx[pp2], pb[bg][:, :], AF.Square)
                        sACT.inc(sc.copy(sux[pp2], pb[bu][:, :]))
                    guev_v = sACT.val
                    bank_ready[bg] = (sACT, sACT.val)
                    bank_ready[bu] = (sACT, sACT.val)

                    @blk.vector
                    def _(v, j=j, pp2=pp2, sbi=sbi):
                        v.wait_ge(sACT.h, guev_v)
                        if sbi > 0 and j == 0:
                            v.wait_ge(sPE.h, dmm_last_sb)
                        v.tensor_scalar(srx[pp2], sqx[pp2], GB, 1.0,
                                        ALU.mult, ALU.add)
                        sDVE.inc(v.tensor_tensor(srx[pp2], sgx[pp2], srx[pp2],
                                                 ALU.mult))
                    p3_v = sDVE.val

                    @blk.scalar
                    def _(sc, pp2=pp2):
                        sc.wait_ge(sDVE.h, p3_v)
                        sACT.inc(sc.activation(srx[pp2], srx[pp2], AF.Tanh,
                                               scale=GA))
                    th_v = sACT.val

                    @blk.vector
                    def _(v, j=j, pp2=pp2):
                        v.wait_ge(sACT.h, th_v)
                        v.tensor_scalar(srx[pp2], srx[pp2], 0.5, 0.5,
                                        ALU.mult, ALU.add)
                        v.tensor_tensor(sgx[pp2], sgx[pp2], sux[pp2], ALU.mult)
                        last = v.tensor_tensor(sinter[:, SEG * j:SEG * (j + 1)],
                                               sgx[pp2], srx[pp2], ALU.mult)
                        sDVE.inc(last)
                    if sbi == 0 and j == 0 and p == 0:
                        pass
                    gu_rd[pp2] = sDVE.val
                    inter_last = sDVE.val

                for mg in range(4):
                    @blk.gpsimd
                    def _(g, p=p, sbi=sbi, mg=mg):
                        if swd_use[0] > 0:
                            g.wait_ge(sPE.h, swd_use[0])
                        for j in range(NT):
                            nt2 = NT * sbi + j
                            g.dma_start(
                                out=swd[:, SEG * j:SEG * (j + 1)],
                                in_=WdD[128 * nt2:128 * (nt2 + 1),
                                        512 * mg:512 * (mg + 1)]
                                ).then_inc(qG.h, 16)
                        qG.val += 16 * NT
                    dwl_v = qG.val

                    for mt in range(4):
                        bd = 4 + (mt % 2)
                        mtg = 4 * mg + mt
                        @blk.tensor
                        def _(t, mg=mg, mt=mt, bd=bd, dwl_v=dwl_v,
                              inter_last=inter_last):
                            if mt == 0:
                                t.wait_ge(qG.h, dwl_v)
                                t.wait_ge(sDVE.h, inter_last)
                            wait_bank(t, bd)
                            for j in range(NT):
                                m = t.matmul(
                                    pb[bd][:, :],
                                    swd[:, SEG * j + 128 * mt:SEG * j + 128 * (mt + 1)],
                                    sinter[:, 512 * j:512 * (j + 1)],
                                    start=(j == 0), stop=(j == NT - 1))
                            sPE.inc(m)
                        dmm_v = sPE.val
                        swd_use[0] = sPE.val

                        @blk.vector
                        def _(v, mtg=mtg, bd=bd, dmm_v=dmm_v, sbi=sbi,
                              h2t_v=h2t_v):
                            v.wait_ge(sPE.h, dmm_v)
                            if sbi == 0:
                                v.wait_ge(qS.h, h2t_v)
                            sDVE.inc(v.tensor_tensor(
                                sacc[:, SEG * mtg:SEG * (mtg + 1)],
                                pb[bd][:, :],
                                sacc[:, SEG * mtg:SEG * (mtg + 1)], ALU.add))
                        bank_ready[bd] = (sDVE, sDVE.val)
                dmm_last_sb = sPE.val
            fin_v = sDVE.val

            for th in range(2):
                for tt in range(2):
                    ttg = 2 * th + tt
                    for mt in range(NT):
                        @blk.tensor
                        def _(t, p=p, ttg=ttg, mt=mt, fin_v=fin_v):
                            if ttg == 0 and mt == 0:
                                t.wait_ge(sDVE.h, fin_v)
                            wait_bank(t, 6)
                            m = t.transpose(
                                pb[6][:, 0:128],
                                sacc[:, SEG * mt + 128 * ttg:SEG * mt + 128 * (ttg + 1)],
                                sident[:])
                            sPE.inc(m)
                        tp_v = sPE.val

                        @blk.vector
                        def _(v, p=p, th=th, tt=tt, mt=mt, tp_v=tp_v):
                            v.wait_ge(sPE.h, tp_v)
                            if th > 0 and tt == 0 and mt == 0:
                                v.wait_ge(sST.h, sST.val)
                            colbase = D * tt + 256 * (mt // 2) + (mt % 2)
                            sDVE.inc(v.tensor_copy(
                                sout[:, colbase:colbase + 255:2], pb[6][:, 0:128]))
                        bank_ready[6] = (sDVE, sDVE.val)
                st_v = sDVE.val

                @blk.gpsimd
                def _(g, p=p, th=th, st_v=st_v):
                    g.wait_ge(sDVE.h, st_v)
                    r0 = 512 * p + 256 * th
                    g.dma_start(out=out_h[r0:r0 + 128, :], in_=sout[:, 0:D]
                                ).then_inc(sST.h, 16)
                    g.dma_start(out=out_h[r0 + 128:r0 + 256, :], in_=sout[:, D:2 * D]
                                ).then_inc(sST.h, 16)
                    sST.val += 32
            st_prev_pass = sST.val
            st_prev_dve = sDVE.val
            gu_mm_prev_pass = sPE.val

        @blk.gpsimd
        def _(g):
            g.wait_ge(sST.h, sST.val)

    return nc


_NC_CACHE = None


def kernel(L0_attn_x, L0_attn_W_in, L0_attn_conv_w, L0_attn_conv_b, L0_attn_W_x,
           L0_attn_W_dt, L0_attn_b_dt, L0_attn_A_log, L0_attn_D, L0_attn_W_out,
           L0_ffn_gate_W_packed, L0_ffn_gate_scales, L0_ffn_up_W_packed,
           L0_ffn_up_scales, L0_ffn_down_W_packed, L0_ffn_down_scales):
    global _NC_CACHE
    x = np.asarray(L0_attn_x)
    d_order = _rho_order()

    WinT = np.ascontiguousarray(np.asarray(L0_attn_W_in).T)[d_order]
    WxT = np.ascontiguousarray(np.asarray(L0_attn_W_x).T)
    WdtT = np.ascontiguousarray(np.asarray(L0_attn_W_dt).T)
    WoutT = np.ascontiguousarray(np.asarray(L0_attn_W_out).T)[:, d_order]
    convw = np.asarray(L0_attn_conv_w)
    convb = np.asarray(L0_attn_conv_b)
    b_dt = np.asarray(L0_attn_b_dt)
    A_log = np.asarray(L0_attn_A_log)
    Dv = np.asarray(L0_attn_D)

    def t16(a_i32):
        v = np.ascontiguousarray(a_i32).view(np.int16)[:, 0::2]
        return np.ascontiguousarray(v.T)

    gpkT = t16(np.asarray(L0_ffn_gate_W_packed))
    upkT = t16(np.asarray(L0_ffn_up_W_packed))
    dpkT = t16(np.asarray(L0_ffn_down_W_packed))
    gscT = np.ascontiguousarray(np.asarray(L0_ffn_gate_scales).T)
    uscT = np.ascontiguousarray(np.asarray(L0_ffn_up_scales).T)
    dscT = np.ascontiguousarray(np.asarray(L0_ffn_down_scales).T)
    dpkT = dpkT[:, d_order]
    dscT = dscT[:, d_order]

    ident = np.eye(128, dtype=np.float32)

    def dmaj(vec_half):
        return np.ascontiguousarray(vec_half.reshape(NT, 128).T)

    in_maps = []
    for c in range(8):
        beta, rho = c // 2, c % 2
        dsl = slice(rho * EH, (rho + 1) * EH)
        xTc = np.ascontiguousarray(x[beta].T[d_order])
        cols = np.r_[rho * EH:(rho + 1) * EH, E + rho * EH:E + (rho + 1) * EH]
        winT_c = np.ascontiguousarray(WinT[:, cols])
        wx = WxT[dsl]
        wxT_c = np.ascontiguousarray(np.concatenate(
            [wx[:, :144], np.zeros((EH, 16), np.float32), wx[:, 144:]], axis=1))
        nsl = slice(2048 * c, 2048 * (c + 1))
        in_maps.append(dict(
            xT=xTc, winT=winT_c, wxT=wxT_c,
            wdtT=np.ascontiguousarray(WdtT[:, dsl]),
            woutT=np.ascontiguousarray(WoutT[dsl]),
            convw=np.ascontiguousarray(
                convw[dsl].reshape(NT, 128, DC).transpose(1, 0, 2).reshape(
                    128, NT * DC)),
            convb=dmaj(convb[dsl]), bdt=dmaj(b_dt[dsl]),
            alog=np.ascontiguousarray(
                A_log[dsl].reshape(NT, 128, S).transpose(1, 0, 2).reshape(
                    128, NT * S)),
            dvec=dmaj(Dv[dsl]),
            gpk=np.ascontiguousarray(gpkT[:, nsl][:, d_order]),
            upk=np.ascontiguousarray(upkT[:, nsl][:, d_order]),
            dpk=np.ascontiguousarray(dpkT[1024 * c:1024 * (c + 1), :]),
            gsc=np.ascontiguousarray(
                np.repeat(gscT[:, nsl][:, d_order], 16, axis=0)),
            usc=np.ascontiguousarray(
                np.repeat(uscT[:, nsl][:, d_order], 16, axis=0)),
            dsc=np.ascontiguousarray(
                np.repeat(dscT[64 * c:64 * (c + 1), :], 16, axis=0)),
            ident=ident,
        ))

    if _NC_CACHE is None:
        _NC_CACHE = build_nc()
    res = run_bass_kernel_spmd(_NC_CACHE, in_maps, core_ids=list(range(8)))

    out = np.empty((B, T, D), dtype=np.float32)
    for c in range(8):
        beta, rho = c // 2, c % 2
        oc = res.results[c]["out"]
        for p in range(NPASS):
            s = 2 * p + rho
            out[beta, SEG * s:SEG * (s + 1), :] = oc[SEG * p:SEG * (p + 1), :]
    return out
